# version 09 (final)
# baseline (speedup 1.0000x reference)
import numpy as np
import ml_dtypes
import concourse.bass as bass
import concourse.mybir as mybir
from concourse.bass_utils import run_bass_kernel_spmd
from contextlib import ExitStack

F32 = mybir.dt.float32
BF16 = mybir.dt.bfloat16
AF = mybir.ActivationFunctionType
ALU = mybir.AluOpType
AX = mybir.AxisListType

KC = 16
DM = 2048
NTOK = 2048
MACRO = 1024
NMACRO = 2
NE = 17
NKV = 8
NQH = 16
NPM = NE * 12
PI = float(np.pi)
MAGIC = 12582912.0
INV2PI = float(1.0 / (2.0 * np.pi))
TWO_PI = float(2.0 * np.pi)
LN_EPS = 1e-5

SEMS = ["s_pe", "s_act", "s_dve",
        "s_xin", "s_st", "s_misc", "s_cast", "s_castp",
        "s_wA", "s_wB", "s_wD", "s_w0", "s_w1"]


class Emit:

    def __init__(self, sems):
        self.sems = sems
        self.snaps = {}
        self.saved = {}

    def start(self, who, eng):
        self.who = who
        self.eng = eng
        self.cnt = dict(self.saved.get(who, {k: 0 for k in self.sems}))

    def finish(self):
        self.saved[self.who] = dict(self.cnt)

    def snap(self, key, sem):
        v = self.cnt[sem]
        old = self.snaps.get(key)
        if old is not None:
            assert old == (sem, v), f"snap {key}: {old} != {(sem, v)}"
        self.snaps[key] = (sem, v)

    def _waits(self, waits):
        for w in waits:
            if isinstance(w, str):
                s, v = self.snaps[w]
            else:
                s, v = w
            if v > 0:
                self.eng.wait_ge(self.sems[s], v)

    def op(self, engname, fn, waits=(), inc=None, n=1, dma=False):
        amt = 16 if dma else n
        if inc is not None:
            self.cnt[inc] += amt
        if self.who != engname:
            return
        self._waits(waits)
        ins = fn(self.eng)
        if inc is not None:
            ins.then_inc(self.sems[inc], amt)

    def wait(self, engname, waits):
        if self.who != engname:
            return
        self._waits(waits)



def _host_prep(inp):
    f32 = np.float32
    H = {}
    gup = np.asarray(inp["L0_ffn_gate_up_proj"], f32)
    dn = np.asarray(inp["L0_ffn_down_proj"], f32)
    shg = np.asarray(inp["L0_ffn_shared_gate_w"], f32)
    shu = np.asarray(inp["L0_ffn_shared_up_w"], f32)
    shd = np.asarray(inp["L0_ffn_shared_down_w"], f32)
    perm = []
    for j in range(6):
        perm += list(range(j * 128, (j + 1) * 128))
        perm += list(range(768 + j * 128, 768 + (j + 1) * 128))
    gup_T = np.empty((NE, DM, 1536), f32)
    for e in range(16):
        gup_T[e] = gup[e].T[:, perm]
    sh = np.empty((DM, 1536), f32)
    sgT, suT = shg.T, shu.T
    for j in range(6):
        sh[:, (2 * j) * 128:(2 * j + 1) * 128] = sgT[:, j * 128:(j + 1) * 128]
        sh[:, (2 * j + 1) * 128:(2 * j + 2) * 128] = suT[:, j * 128:(j + 1) * 128]
    gup_T[16] = sh
    H["gup_T"] = np.ascontiguousarray(gup_T)
    dn_T = np.empty((NE, 768, DM), f32)
    for e in range(16):
        dn_T[e] = dn[e].T
    dn_T[16] = shd.T
    H["dn_T"] = np.ascontiguousarray(dn_T)

    H["router_T"] = np.ascontiguousarray(np.asarray(inp["L0_ffn_router_w"], f32).T)
    H["wq_T"] = np.ascontiguousarray(np.asarray(inp["L0_attn_Wq"], f32).T)
    H["wk_T"] = np.ascontiguousarray(np.asarray(inp["L0_attn_Wk"], f32).T)
    H["wv_T"] = np.ascontiguousarray(np.asarray(inp["L0_attn_Wv"], f32).T)
    H["wo_T"] = np.ascontiguousarray(np.asarray(inp["L0_attn_Wo"], f32).T)
    H["r_T"] = np.ascontiguousarray(np.asarray(inp["L0_attn_rand_matrix"], f32).T)

    def fm_bias(b, nch):
        return np.ascontiguousarray(np.asarray(b, f32).reshape(nch, 128).T)
    H["bq_fm"] = fm_bias(inp["L0_attn_bq"], 16)
    H["bk_fm"] = fm_bias(inp["L0_attn_bk"], 8)
    H["bq_sh"] = np.ascontiguousarray(np.roll(H["bq_fm"], -64, axis=0))
    H["bk_sh"] = np.ascontiguousarray(np.roll(H["bk_fm"], -64, axis=0))
    H["bv_fm"] = fm_bias(inp["L0_attn_bv"], 8)
    H["bo_fm"] = fm_bias(inp["L0_attn_bo"], 16)
    H["ebias_t"] = np.ascontiguousarray(
        np.broadcast_to(np.asarray(inp["L0_ffn_e_bias"], f32), (128, 16)))

    H["ident"] = np.eye(128, dtype=f32)
    H["onesf"] = np.ones((128, 1), f32)
    H["onesf_row"] = np.ones((1, 128), f32)
    H["onesbf_col"] = np.ones((128, 1), f32)
    H["onesbf_row"] = np.ones((1, 128), f32)
    oh = np.zeros((16, 2048), f32)
    for e in range(16):
        oh[e, e * 128:(e + 1) * 128] = 1
    H["onehot"] = oh
    H["desc_t"] = np.ascontiguousarray(
        np.broadcast_to((16.0 - np.arange(16, dtype=f32)), (128, 16)))
    H["ones128"] = np.ones((128, 128), f32)
    H["eps12"] = np.full((1, 1), 1e-12, f32)
    H["ebias8"] = np.ascontiguousarray(np.tile(
        np.asarray(inp["L0_ffn_e_bias"], f32)[None, :], (128, 8)))
    H["desc8"] = np.ascontiguousarray(np.tile(
        (16.0 - np.arange(16, dtype=f32))[None, :], (128, 8)))

    invf = (1.0 / (10000.0 ** (np.arange(0, 128, 2, dtype=f32) / 128.0))).astype(f32)
    freqs = (np.arange(NTOK, dtype=f32)[:, None] * invf[None, :]).astype(f32)
    emb = np.concatenate([freqs, freqs], axis=1)
    H["cos_fm"] = np.ascontiguousarray(np.cos(emb).astype(f32).T)
    H["sin_fm"] = np.ascontiguousarray(np.sin(emb).astype(f32).T)
    H["sin_sh"] = np.ascontiguousarray(np.roll(H["sin_fm"], -64, axis=0))
    srh = H["sin_fm"].copy()
    srh[0:64] = -srh[0:64]
    H["sin_rh"] = np.ascontiguousarray(srh)
    return H



def build_nc(phases=("A", "B", "C"), dbg=False):
    nc = bass.Bass("TRN2", target_bir_lowering=False)
    D = {}
    D["x_fm"] = nc.dram_tensor("x_fm", [DM, NTOK], F32, kind="ExternalInput").ap()
    D["out_fm"] = nc.dram_tensor("out_fm", [DM, NTOK], F32, kind="ExternalOutput").ap()
    if dbg:
        D["dbg_y"] = nc.dram_tensor("dbg_y", [DM, NTOK], F32,
                                    kind="ExternalOutput").ap()
        D["dbg_comb"] = nc.dram_tensor("dbg_comb", [16, NTOK], F32,
                                       kind="ExternalOutput").ap()
        D["dbg_sc"] = nc.dram_tensor("dbg_sc", [128, 16 * 16], F32,
                                     kind="ExternalOutput").ap()
    D["x1s"] = nc.dram_tensor("x1s", [DM, NTOK], F32, kind="Internal").ap()
    D["x2s"] = nc.dram_tensor("x2s", [DM, NTOK], F32, kind="Internal").ap()
    for name, shape in [
        ("gup_T", [NE, DM, 1536]), ("dn_T", [NE, 768, DM]),
        ("router_T", [DM, 16]), ("wq_T", [DM, DM]), ("wk_T", [DM, 1024]),
        ("wv_T", [DM, 1024]), ("wo_T", [DM, DM]), ("r_T", [128, 64]),
        ("bq_fm", [128, 16]), ("bk_fm", [128, 8]), ("bv_fm", [128, 8]),
        ("bq_sh", [128, 16]), ("bk_sh", [128, 8]),
        ("bo_fm", [128, 16]), ("ebias_t", [128, 16]), ("ident", [128, 128]),
        ("onesf", [128, 1]), ("onesf_row", [1, 128]), ("desc_t", [128, 16]),
        ("ones128", [128, 128]), ("ebias8", [128, 128]), ("desc8", [128, 128]),
        ("eps12", [1, 1]),
        ("cos_fm", [128, NTOK]), ("sin_fm", [128, NTOK]), ("sin_rh", [128, NTOK]),
    ]:
        D[name] = nc.dram_tensor(name, shape, F32, kind="ExternalInput").ap()
    for name, shape in [
        ("onesbf_col", [128, 1]), ("onesbf_row", [1, 128]), ("onehot", [16, 2048]),
    ]:
        D[name] = nc.dram_tensor(name, shape, F32, kind="ExternalInput").ap()

    sem_es = ExitStack()
    sems = {s: sem_es.enter_context(nc.semaphore(s)) for s in SEMS}
    E = Emit(sems)

    pes = ExitStack()
    P = {}
    def psb(name, shape, dt):
        P[name] = pes.enter_context(nc.sbuf_tensor("sb_" + name, shape, dt))
    psb("ident", [128, 128], F32)
    psb("onesf", [128, 1], F32)
    psb("onesf_row", [1, 128], F32)
    psb("onesbf_col", [128, 1], BF16)
    psb("onesbf_row", [1, 128], BF16)
    psb("oh_bf", [16, 2048], BF16)
    psb("bq", [128, 16], F32)
    psb("bk", [128, 8], F32)
    psb("bq_sh", [128, 16], F32)
    psb("bk_sh", [128, 8], F32)
    psb("bv", [128, 8], F32)
    psb("bo", [128, 16], F32)
    psb("router_bf", [128, KC * 16], BF16)
    psb("rT_bf", [128, 64], BF16)
    psb("ones128", [128, 128], BF16)
    psb("ebias8", [128, 128], F32)
    psb("eps12", [1, 1], F32)
    psb("desc8", [128, 128], BF16)
    banks = [pes.enter_context(nc.psum_tensor(f"bank{i}", [128, 512], F32))
             for i in range(8)]

    def run_block(blk, schedule):
        engmap = [("sp", blk.sync), ("pool", blk.gpsimd), ("pe", blk.tensor),
                  ("act", blk.scalar), ("dve", blk.vector)]
        for who, starter in engmap:
            def runner(eng, who=who):
                E.start(who, eng)
                schedule()
                E.finish()
            starter(runner)

    def sched_prologue():
        pairs = [("ident", "ident"), ("onesf", "onesf"), ("onesf_row", "onesf_row"),
                 ("bq", "bq_fm"), ("bk", "bk_fm"), ("bv", "bv_fm"), ("bo", "bo_fm"),
                 ("bq_sh", "bq_sh"), ("bk_sh", "bk_sh"),
                 ("ebias8", "ebias8"), ("eps12", "eps12")]
        for dst, src in pairs:
            E.op("sp", lambda sy, d=dst, s=src: sy.dma_start(out=P[d][:], in_=D[s]),
                 inc="s_cast", dma=True)
        for dst, src in [("onesbf_col", "onesbf_col"), ("onesbf_row", "onesbf_row"),
                         ("oh_bf", "onehot"), ("ones128", "ones128"),
                         ("desc8", "desc8")]:
            E.op("pool", lambda g, d=dst, s=src: g.dma_start(out=P[d][:], in_=D[s]),
                 inc="s_castp", dma=True)
        E.op("pool", lambda g: g.dma_start(
            out=P["router_bf"][:].rearrange("p (k n) -> p k n", k=KC),
            in_=D["router_T"].rearrange("(k p) n -> p k n", p=128)),
            inc="s_castp", dma=True)
        E.op("pool", lambda g: g.dma_start(out=P["rT_bf"][:], in_=D["r_T"]),
             inc="s_castp", dma=True)
        E.snap("pro_sp", "s_cast")
        E.snap("pro_pool", "s_castp")

    with nc.Block("prologue") as blk:
        run_block(blk, sched_prologue)

    def em_ln(tag, xs_view, y_view, nts, T, waits0):
        rows, scr2 = T["rows"], T["scr2"]
        first = {k: list(v) for k, v in waits0.items()}
        def w0(eng):
            return first.pop(eng, [])
        sl = lambda i: rows[0:1, i * 512:(i + 1) * 512]
        for ts in range(nts):
            pe_extra = [f"{tag}y{ts-1}"] if ts > 0 else []
            for c in range(KC):
                E.op("act", lambda s, c=c, ts=ts: s.activation(
                    scr2[:, (c % 2) * 512:(c % 2) * 512 + 512], xs_view(c, ts),
                    AF.Square),
                    waits=w0("act") + ([f"{tag}sqm{ts}_{c-2}"] if c >= 2 else []),
                    inc="s_act")
                E.snap(f"{tag}sq{ts}_{c}", "s_act")
                E.op("pe", lambda t, c=c, ts=ts: t.matmul(
                    banks[6][0:1, :], P["onesf"][:], xs_view(c, ts),
                    start=(c == 0), stop=(c == KC - 1)),
                    waits=(w0("pe") + pe_extra) if c == 0 else [])
                E.op("pe", lambda t, c=c: t.matmul(
                    banks[7][0:1, :], P["onesf"][:],
                    scr2[:, (c % 2) * 512:(c % 2) * 512 + 512],
                    start=(c == 0), stop=(c == KC - 1)),
                    waits=[f"{tag}sq{ts}_{c}"], inc="s_pe")
                E.snap(f"{tag}sqm{ts}_{c}", "s_pe")
            E.snap(f"{tag}stat{ts}", "s_pe")
            E.op("dve", lambda v: v.tensor_scalar_mul(sl(0), banks[6][0:1, :], 1.0 / DM),
                 waits=w0("dve") + [f"{tag}stat{ts}"], inc="s_dve")
            E.op("dve", lambda v: v.tensor_scalar_mul(sl(1), banks[7][0:1, :], 1.0 / DM))
            E.op("dve", lambda v: v.tensor_mul(sl(2), sl(0), sl(0)))
            E.op("dve", lambda v: v.tensor_sub(sl(1), sl(1), sl(2)))
            E.op("dve", lambda v: v.tensor_single_scalar(sl(1), sl(1), LN_EPS, ALU.add),
                 inc="s_dve")
            E.snap(f"{tag}var{ts}", "s_dve")
            E.op("act", lambda s: s.activation(sl(2), sl(1), AF.Ln),
                 waits=[f"{tag}var{ts}"])
            E.op("act", lambda s: s.activation(sl(1), sl(2), AF.Exp, scale=-0.5),
                 inc="s_act")
            E.snap(f"{tag}rstd{ts}", "s_act")
            E.op("dve", lambda v: v.scalar_tensor_tensor(
                sl(0), sl(0), -1.0, sl(1), ALU.mult, ALU.mult),
                waits=[f"{tag}rstd{ts}"], inc="s_dve")
            E.snap(f"{tag}brow{ts}", "s_dve")
            E.op("pe", lambda t: t.matmul(banks[6][:, :], P["onesf_row"][:], sl(1),
                                          start=True, stop=True),
                 waits=[f"{tag}brow{ts}"])
            E.op("pe", lambda t: t.matmul(banks[7][:, :], P["onesf_row"][:], sl(0),
                                          start=True, stop=True), inc="s_pe")
            E.snap(f"{tag}bc{ts}", "s_pe")
            for c in range(KC):
                E.op("dve", lambda v, c=c, ts=ts: v.tensor_mul(
                    T["t1"][:], xs_view(c, ts), banks[6][:, :]),
                    waits=([f"{tag}bc{ts}"] if c == 0 else []))
                E.op("dve", lambda v, c=c, ts=ts: v.tensor_add(
                    y_view(c, ts), T["t1"][:], banks[7][:, :]), inc="s_dve")
            E.snap(f"{tag}y{ts}", "s_dve")
        E.snap(f"{tag}ydone", "s_dve")

    def alloc_ffn(es, pref):
        T = {}
        def sb(name, shape, dt):
            T[name] = es.enter_context(nc.sbuf_tensor(pref + name, shape, dt))
        sb("xs", [128, KC * MACRO], F32)
        sb("y", [128, KC * MACRO], BF16)
        sb("wA", [128, KC * 768], BF16)
        sb("wB", [128, KC * 768], BF16)
        sb("wD", [128, 6 * DM], BF16)
        sb("h", [128, 6 * MACRO], BF16)
        sb("c_sb", [128, 2 * MACRO], BF16)
        sb("scr2", [128, 2 * 512], F32)
        sb("t1", [128, 512], F32)
        sb("rows", [1, 3 * 512], F32)
        sb("scores", [128, 8 * 16], F32)
        sb("tk", [128, 3 * 128 + 32], F32)
        sb("combine_bf", [16, MACRO], BF16)
        return T

    def sched_ffn(tag, T, src, dst, start_waits):
        src3 = src.rearrange("(c p) t -> p c t", p=128)
        dst3 = dst.rearrange("(c p) t -> p c t", p=128)
        for eng in ("sp", "pool", "pe", "act", "dve"):
            E.wait(eng, list(start_waits) +
                   [("s_cast", E.snaps["pro_sp"][1]),
                    ("s_castp", E.snaps["pro_pool"][1])])
        for m in range(NMACRO):
            mt = f"{tag}{m}"
            pv = f"{tag}{m-1}"
            E.op("sp", lambda sy, m=m: sy.dma_start(
                out=T["xs"][:].rearrange("p (c t) -> p c t", c=KC),
                in_=src3[:, :, m * MACRO:(m + 1) * MACRO]),
                waits=([f"{pv}st"] if m > 0 else []),
                inc="s_xin", dma=True)
            E.snap(f"{mt}xl", "s_xin")

            xs_view = lambda c, ts: T["xs"][:, c * MACRO + ts * 512:
                                            c * MACRO + ts * 512 + 512]
            y_view = lambda c, ts: T["y"][:, c * MACRO + ts * 512:
                                          c * MACRO + ts * 512 + 512]
            w0 = {"pe": [f"{mt}xl"] + ([f"{pv}dnevL"] if m > 0 else []),
                  "act": [f"{mt}xl"] + ([f"{pv}sgfree"] if m > 0 else []),
                  "dve": [f"{mt}xl"] +
                         ([f"{pv}e{NE-1}guwBdone"] if m > 0 else [])}
            em_ln(mt, xs_view, y_view, 2, T, w0)

            for tc in range(8):
                bk = banks[6 + (tc % 2)]
                rw = [f"{mt}ydone"] + ([f"{mt}sig{tc-2}"] if tc >= 2 else [])
                for k in range(KC):
                    E.op("pe", lambda t, tc=tc, k=k, bk=bk: t.matmul(
                        bk[:, 0:16],
                        T["y"][:, k * MACRO + tc * 128: k * MACRO + tc * 128 + 128],
                        P["router_bf"][:, k * 16:(k + 1) * 16],
                        start=(k == 0), stop=(k == KC - 1)),
                        waits=(rw if k == 0 else []),
                        inc=("s_pe" if k == KC - 1 else None))
                E.snap(f"{mt}rt{tc}", "s_pe")
                E.op("act", lambda s, tc=tc, bk=bk: s.activation(
                    T["scores"][:, tc * 16:(tc + 1) * 16], bk[:, 0:16], AF.Sigmoid),
                    waits=[f"{mt}rt{tc}", f"{mt}ydone"], inc="s_act")
                E.snap(f"{mt}sig{tc}", "s_act")
            tk = T["tk"]
            seg = lambda ap: ap.rearrange("p (s e) -> p s e", e=16)
            bsc = tk[:, 0:128]; msk = tk[:, 128:256]; val = tk[:, 256:384]
            cw = tk[:, 0:128]; comb8 = tk[:, 256:384]
            m1 = tk[:, 384:392]; ws = tk[:, 392:400]; iv = tk[:, 400:408]
    
            def dsync(fn, waits=()):
                E.op("dve", fn,
                     waits=list(waits) + [("s_dve", E.cnt["s_dve"])],
                     inc="s_dve")
            bc = lambda ap: ap.unsqueeze(2).broadcast_to([128, 8, 16])
            tw = [f"{mt}sig7"]
            if m > 0:
                tw.append(f"{pv}ctpL")
            dsync(lambda v: v.tensor_add(bsc, T["scores"][:], P["ebias8"][:]),
                  waits=tw)
            dsync(lambda v: v.memset(msk, 0.0))
            for it in range(4):
                dsync(lambda v: v.reduce_max(m1, seg(bsc), axis=AX.X))
                dsync(lambda v: v.tensor_tensor(seg(val), seg(bsc), bc(m1),
                                                ALU.is_ge))
                dsync(lambda v: v.tensor_mul(val, val, P["desc8"][:]))
                dsync(lambda v: v.reduce_max(m1, seg(val), axis=AX.X))
                dsync(lambda v: v.tensor_tensor(seg(val), seg(val), bc(m1),
                                                ALU.is_equal))
                dsync(lambda v: v.tensor_add(msk, msk, val))
                dsync(lambda v: v.scalar_tensor_tensor(
                    bsc, val, -1e30, bsc, ALU.mult, ALU.add))
            dsync(lambda v: v.tensor_mul(cw, msk, T["scores"][:]))
            dsync(lambda v: v.reduce_sum(ws, seg(cw), axis=AX.X))
            dsync(lambda v: v.tensor_single_scalar(ws, ws, 1e-20, ALU.add))
            dsync(lambda v: v.reciprocal(iv, ws))
            dsync(lambda v: v.scalar_tensor_tensor(
                seg(comb8), seg(cw), 0.5, bc(iv), ALU.mult, ALU.mult))
            E.snap(f"{mt}comb", "s_dve")
            for tc in range(8):
                ctw = [f"{mt}comb"]
                if tc >= 1:
                    ctw.append(f"{mt}cbf{tc-1}")
                elif m > 0:
                    ctw += [f"{pv}silu{NPM-1}", f"{pv}hp{NPM-1}"]
                E.op("pe", lambda t, tc=tc: t.transpose(
                    banks[tc // 2][0:16, (tc % 2) * 128:(tc % 2) * 128 + 128],
                    comb8[:, tc * 16:(tc + 1) * 16], P["ident"][:]),
                    waits=ctw, inc="s_pe")
                E.snap(f"{mt}ctp{tc}", "s_pe")
                E.op("act", lambda s, tc=tc: s.activation(
                    T["combine_bf"][0:16, tc * 128:(tc + 1) * 128],
                    banks[tc // 2][0:16, (tc % 2) * 128:(tc % 2) * 128 + 128],
                    AF.Copy),
                    waits=[f"{mt}ctp{tc}"], inc="s_act")
                E.snap(f"{mt}cbf{tc}", "s_act")
            E.snap(f"{mt}ctpL", "s_pe")
            E.snap(f"{mt}cbfL", "s_act")
            if dbg and tag == "A":
                E.op("pool", lambda g, m=m: g.dma_start(
                    out=D["dbg_y"].rearrange("(c p) t -> p c t", p=128)
                    [:, :, m * MACRO:(m + 1) * MACRO],
                    in_=T["y"][:].rearrange("p (c t) -> p c t", c=KC)),
                    waits=[f"{mt}ydone"], inc="s_misc", dma=True)
                E.op("pool", lambda g, m=m: g.dma_start(
                    out=D["dbg_comb"][0:16, m * MACRO:(m + 1) * MACRO],
                    in_=T["combine_bf"][0:16, :]),
                    waits=[f"{mt}cbfL"], inc="s_misc", dma=True)
                E.op("pool", lambda g, m=m: g.dma_start(
                    out=D["dbg_sc"][:, m * 128:(m + 1) * 128],
                    in_=T["scores"][:]),
                    waits=[f"{mt}cbfL"], inc="s_misc", dma=True)
                E.snap(f"{mt}dbg", "s_misc")
                E.wait("sp", [f"{mt}dbg"])

            pair_ctr = 0
            for e in range(NE):
                et = f"{mt}e{e}"
                for role, semn, cols in (("wA", "s_wA", (0, 768)),
                                         ("wB", "s_wB", (768, 1536))):
                    kp = (f"{mt}e{e-1}gu{role}done" if e > 0 else
                          (f"{pv}e{NE-1}gu{role}done" if m > 0 else None))
                    E.op("pool", lambda g, e=e, cols=cols, role=role: g.dma_start(
                        out=T[role][:].rearrange("p (k n) -> p k n", k=KC),
                        in_=D["gup_T"][e, :, cols[0]:cols[1]].rearrange(
                            "(k p) n -> p k n", p=128)),
                        waits=([kp] if kp else []), inc=semn, dma=True)
                    E.snap(f"{et}{role}ld", semn)
                kp = (f"{mt}e{e-1}dnL" if e > 0 else
                      (f"{pv}e{NE-1}dnL" if m > 0 else None))
                E.op("pool", lambda g, e=e: g.dma_start(
                    out=T["wD"][:].rearrange("p (k n) -> p k n", k=6),
                    in_=D["dn_T"][e].rearrange("(k p) n -> p k n", p=128)),
                    waits=([kp] if kp else []), inc="s_wD", dma=True)
                E.snap(f"{et}wDld", "s_wD")

                if e < 16:
                    for ts in range(2):
                        if ts == 0:
                            cwaits = [f"{mt}cbfL"]
                            if e > 0:
                                cwaits.append(f"{mt}e{e-1}dnevL")
                            elif m > 0:
                                cwaits.append(f"{pv}dnevL")
                        else:
                            cwaits = [f"{et}cev0"]
                        E.op("pe", lambda t, e=e, ts=ts: t.matmul(
                            banks[6][:, :], P["oh_bf"][0:16, e * 128:(e + 1) * 128],
                            T["combine_bf"][0:16, ts * 512:(ts + 1) * 512],
                            start=True, stop=True),
                            waits=cwaits, inc="s_pe")
                        E.snap(f"{et}cmm{ts}", "s_pe")
                        hw = [f"{et}cmm{ts}"]
                        if e >= 2:
                            hw.append(f"{mt}e{e-2}hL")
                        elif m > 0:
                            hw.append(f"{pv}e{14 + e}hL")
                        E.op("act", lambda s, e=e, ts=ts: s.activation(
                            T["c_sb"][:, (e % 2) * MACRO + ts * 512:
                                      (e % 2) * MACRO + ts * 512 + 512],
                            banks[6][:, :], AF.Copy),
                            waits=hw, inc="s_act")
                        E.snap(f"{et}cev{ts}", "s_act")

                for gi, (hi, ts) in enumerate([(0, 0), (0, 1), (1, 0), (1, 1)]):
                    role = "wA" if hi == 0 else "wB"
                    gwaits = [f"{et}{role}ld", f"{mt}ydone"]
                    if pair_ctr >= 3:
                        gwaits += [f"{mt}silu{pair_ctr-1}", f"{mt}hp{pair_ctr-1}"]
                    elif m > 0:
                        gwaits += [f"{pv}silu{NPM-1}", f"{pv}hp{NPM-1}"]
                    if e == 0 and gi == 0:
                        gwaits.append(f"{mt}cbfL")
                    for ofp in range(6):
                        for k in range(KC):
                            E.op("pe", lambda t, role=role, ofp=ofp, k=k, ts=ts:
                                 t.matmul(
                                     banks[ofp][:, :],
                                     T[role][:, k * 768 + ofp * 128:
                                             k * 768 + (ofp + 1) * 128],
                                     T["y"][:, k * MACRO + ts * 512:
                                            k * MACRO + ts * 512 + 512],
                                     start=(k == 0), stop=(k == KC - 1)),
                                 waits=(gwaits if (ofp == 0 and k == 0) else []),
                                 inc=("s_pe" if k == KC - 1 else None))
                        E.snap(f"{et}gu{gi}_{ofp}", "s_pe")
                    if hi == 0 and ts == 1:
                        E.snap(f"{et}guwAdone", "s_pe")
                    if hi == 1 and ts == 1:
                        E.snap(f"{et}guwBdone", "s_pe")
                    for j in range(3):
                        pj = pair_ctr
                        pair_ctr += 1
                        swaits = [f"{et}gu{gi}_{2*j}"]
                        if pj >= 2:
                            swaits.append(f"{mt}hp{pj-2}")
                        E.op("act", lambda s, pj=pj, j=j: s.activation(
                            T["scr2"][:, (pj % 2) * 512:(pj % 2) * 512 + 512],
                            banks[2 * j][:, :], AF.Silu),
                            waits=swaits, inc="s_act")
                        E.snap(f"{mt}silu{pj}", "s_act")
                        hwaits = [f"{mt}silu{pj}", f"{et}gu{gi}_{2*j+1}"]
                        if e < 16:
                            hwaits.append(f"{et}cev1")
                            E.op("dve", lambda v, pj=pj, e=e, ts=ts: v.tensor_mul(
                                T["t1"][:],
                                T["scr2"][:, (pj % 2) * 512:(pj % 2) * 512 + 512],
                                T["c_sb"][:, (e % 2) * MACRO + ts * 512:
                                          (e % 2) * MACRO + ts * 512 + 512]),
                                waits=hwaits)
                        else:
                            E.op("dve", lambda v, pj=pj: v.tensor_scalar_mul(
                                T["t1"][:],
                                T["scr2"][:, (pj % 2) * 512:(pj % 2) * 512 + 512],
                                0.5),
                                waits=hwaits)
                        hcol = (hi * 3 + j) * MACRO + ts * 512
                        E.op("dve", lambda v, hcol=hcol, j=j: v.tensor_mul(
                            T["h"][:, hcol:hcol + 512], T["t1"][:],
                            banks[2 * j + 1][:, :]),
                            inc="s_dve")
                        E.snap(f"{mt}hp{pj}", "s_dve")
                E.snap(f"{mt}e{e}hL", "s_dve")

                for ts in range(2):
                    for of in range(16):
                        didx = e * 32 + ts * 16 + of
                        bk = banks[6 + (didx % 2)]
                        dwaits = [f"{et}wDld", f"{mt}e{e}hL"]
                        if ts == 0 and of == 0 and e < 16:
                            dwaits.append(f"{et}cev1")
                        if didx >= 2:
                            dwaits.append(f"{mt}dnev{didx-2}")
                        elif m > 0:
                            dwaits.append(f"{pv}dnevL")
                        for j in range(6):
                            E.op("pe", lambda t, bk=bk, j=j, of=of, ts=ts: t.matmul(
                                bk[:, :],
                                T["wD"][:, j * DM + of * 128: j * DM + (of + 1) * 128],
                                T["h"][:, j * MACRO + ts * 512:
                                       j * MACRO + ts * 512 + 512],
                                start=(j == 0), stop=(j == 5)),
                                waits=(dwaits if j == 0 else []),
                                inc=("s_pe" if j == 5 else None))
                        E.snap(f"{mt}dnb{didx}", "s_pe")
                        E.op("dve", lambda v, bk=bk, of=of, ts=ts: v.tensor_add(
                            T["xs"][:, of * MACRO + ts * 512:
                                    of * MACRO + ts * 512 + 512],
                            T["xs"][:, of * MACRO + ts * 512:
                                    of * MACRO + ts * 512 + 512],
                            bk[:, :]),
                            waits=[f"{mt}dnb{didx}"], inc="s_dve")
                        E.snap(f"{mt}dnev{didx}", "s_dve")
                E.snap(f"{et}dnL", "s_pe")
                E.snap(f"{mt}e{e}dnevL", "s_dve")
            E.snap(f"{mt}dnevL", "s_dve")
            E.snap(f"{mt}sgfree", "s_dve")
            E.op("sp", lambda sy, m=m: sy.dma_start(
                out=dst3[:, :, m * MACRO:(m + 1) * MACRO],
                in_=T["xs"][:].rearrange("p (c t) -> p c t", c=KC)),
                waits=[f"{mt}dnevL"], inc="s_st", dma=True)
            E.snap(f"{mt}st", "s_st")
        E.snap(f"{tag}end", "s_st")

    def alloc_attn_slices(es):
        T = {}
        def sb(name, shape, dt):
            T[name] = es.enter_context(nc.sbuf_tensor("at_" + name, shape, dt))
        sb("x1s_sb", [128, KC * 512], F32)
        sb("y2", [128, KC * 512], BF16)
        sb("kp", [128, NKV * 512], BF16)
        sb("v_sb", [128, NKV * 512], BF16)
        sb("krotbf", [128, 16 * 512], BF16)
        sb("wu0", [128, KC * 128], BF16)
        sb("wu1", [128, KC * 128], BF16)
        sb("dscr", [128, 512], F32)
        sb("krot", [128, 2 * 512], BF16)
        sb("scrB", [128, 512], F32)
        sb("sq2", [128, 512], F32)
        sb("brow", [1, 3 * 512], F32)
        sb("rngs", [64, 6 * 512], F32)
        sb("p_sb", [64, 2 * 512], BF16)
        sb("prod", [128, 2 * 512], BF16)
        sb("outacc", [128, 2 * 512], F32)
        sb("scr2", [128, 2 * 512], F32)
        sb("rows", [1, 3 * 512], F32)
        sb("cos_sl", [128, 512], BF16)
        sb("sin_sl", [128, 512], BF16)
        T["t1"] = T["dscr"]
        return T

    def sched_attn_slices(T, Z):
        sw = [("s_st", E.snaps["Aend"][1])] if "Aend" in E.snaps else []
        for eng in ("sp", "pool", "pe", "act", "dve"):
            E.wait(eng, sw + [("s_cast", E.snaps["pro_sp"][1]),
                              ("s_castp", E.snaps["pro_pool"][1])])
        x13 = D["x1s"].rearrange("(c p) t -> p c t", p=128)

        def proj_unit(st, pvs, s0, pu, kind, idx, rp, slot):
            ut = f"{st}u{pu}"
            par = pu % 2
            wdram = {"k": "wk_T", "v": "wv_T", "q": "wq_T"}[kind]
            wsem = "s_w0" if par == 0 else "s_w1"
            lw = []
            if pu >= 2:
                lw.append(f"{st}u{pu-2}proj")
            elif pvs is not None:
                lw.append(f"{pvs}u{30+pu}proj")
            E.op("pool", lambda g: g.dma_start(
                out=T[f"wu{par}"][:].rearrange("p (k n) -> p k n", k=KC),
                in_=D[wdram][:, idx * 128:(idx + 1) * 128].rearrange(
                    "(k p) n -> p k n", p=128)),
                waits=lw, inc=wsem, dma=True)
            E.snap(f"{ut}wl", wsem)
            pw = [f"{ut}wl", f"{st}ydone"]
            if pu >= 2:
                pw.append(f"{st}u{pu-2}c")
            elif pvs is not None:
                pw.append(f"{pvs}u{30+pu}c")
            for k in range(KC):
                E.op("pe", lambda t, k=k: t.matmul(
                    banks[par][:, :],
                    T[f"wu{par}"][:, k * 128:(k + 1) * 128],
                    T["y2"][:, k * 512:(k + 1) * 512],
                    start=(k == 0), stop=(k == KC - 1)),
                    waits=(pw if k == 0 else []),
                    inc=("s_pe" if k == KC - 1 else None))
            E.snap(f"{ut}proj", "s_pe")
            if kind == "v":
                vw = [f"{ut}proj"]
                if pvs is not None:
                    vw.append(f"{pvs}fmaL")
                E.op("act", lambda s_: s_.activation(
                    T["v_sb"][:, idx * 512:(idx + 1) * 512], banks[par][:, :],
                    AF.Identity, bias=P["bv"][:, idx:idx + 1]),
                    waits=vw, inc="s_act")
                E.snap(f"{ut}c", "s_act")
                return
            bt = P["bk"] if kind == "k" else P["bq"]
            bsh = P["bk_sh"] if kind == "k" else P["bq_sh"]
            bias = bt[:, idx:idx + 1]
            rt = f"{st}r{rp}"
            rpar = rp % 2
            krv = T["krot"][:, rpar * 512:rpar * 512 + 512]
            cs, sn = T["cos_sl"][:], T["sin_sl"][:]
            rw = [f"{ut}proj", f"{st}tab"]
            if rp >= 2:
                rw.append(f"{st}r{rp-2}sq")
            E.op("dve", lambda v: v.scalar_tensor_tensor(
                T["dscr"][:], banks[par][:, :], bias, cs, ALU.add, ALU.mult),
                waits=rw)
            E.op("dve", lambda v: v.scalar_tensor_tensor(
                T["scrB"][0:64, :], banks[par][64:128, :], bsh[0:64, idx:idx + 1],
                sn[0:64, :], ALU.add, ALU.mult))
            E.op("dve", lambda v: v.scalar_tensor_tensor(
                T["scrB"][64:128, :], banks[par][0:64, :], bsh[64:128, idx:idx + 1],
                sn[64:128, :], ALU.add, ALU.mult))
            E.op("dve", lambda v: v.tensor_add(krv, T["dscr"][:], T["scrB"][:]),
                inc="s_dve")
            E.snap(f"{rt}rope", "s_dve")
            E.snap(f"{ut}c", "s_dve")
            sw2 = []
            if rp >= 1:
                sw2.append(f"{st}r{rp-1}nmm")
            E.op("dve", lambda v: v.tensor_mul(T["sq2"][:], krv, krv),
                 waits=sw2, inc="s_dve")
            E.snap(f"{rt}sq", "s_dve")
            nbi = 2 if (rp >= 16 or rp % 2 == 0) else 4
            nb = banks[nbi]
            nw = [f"{rt}sq"]
            prev_same = rp - 1 if rp >= 16 else rp - 2
            if prev_same >= 0:
                nw.append(f"{st}r{prev_same}xhat")
            if nbi == 4 and rp < 2 and pvs is not None:
                nw.append(f"{pvs}fmaL")
            E.op("pe", lambda t, nb=nb: t.matmul(
                nb[0:1, :], P["onesf"][:], T["sq2"][:],
                start=True, stop=True),
                waits=nw, inc="s_pe")
            E.snap(f"{rt}nmm", "s_pe")
            lint = T["brow"][0:1, 0:512]
            brw = T["brow"][0:1, (1 + rp % 2) * 512:(1 + rp % 2) * 512 + 512]
            bw = [f"{rt}nmm"]
            if rp >= 2:
                bw.append(f"{st}r{rp-2}ibc")
            E.op("act", lambda s_, nb=nb, lint=lint: s_.activation(
                lint, nb[0:1, :], AF.Ln, bias=P["eps12"][0:1, 0:1]),
                waits=bw)
            E.op("act", lambda s_, brw=brw, lint=lint: s_.activation(
                brw, lint, AF.Exp, scale=-0.5), inc="s_act")
            E.snap(f"{rt}inv", "s_act")
            E.op("pe", lambda t, nb=nb, brw=brw: t.matmul(
                nb[:, :], P["onesf_row"][:], brw,
                start=True, stop=True),
                waits=[f"{rt}inv"], inc="s_pe")
            E.snap(f"{rt}ibc", "s_pe")
            xw = [f"{rt}ibc"]
            if slot < 8:
                if pvs is not None and rp < 2:
                    xw.append(f"{pvs}pmmL")
            else:
                if rp >= 16:
                    xw.append(f"{st}q{slot-8}prodL")
                elif rp >= 8 and pvs is not None:
                    xw.append(f"{pvs}q{slot}prodL")
            E.op("dve", lambda v, nb=nb: v.tensor_mul(
                T["krotbf"][:, slot * 512:(slot + 1) * 512], krv, nb[:, :]),
                waits=xw, inc="s_dve")
            E.snap(f"{rt}xhat", "s_dve")

        def range_sin(ut, fpar, sin_dst, sin_waits, cos_dst, prev_sinB,
                      prev_rc):
            pv = T["p_sb"][:, fpar * 512:(fpar + 1) * 512]
            ev = [f"{ut}pmm"] + ([prev_rc] if prev_rc else [])
            E.op("act", lambda s_: s_.activation(pv, banks[3][0:64, :], AF.Copy),
                 waits=ev, inc="s_act")
            E.snap(f"{ut}pev", "s_act")
            S = lambda i: T["rngs"][:, (fpar * 3 + i) * 512:
                                    (fpar * 3 + i + 1) * 512]
            E.op("dve", lambda v: v.tensor_scalar_mul(S(0), pv, INV2PI),
                 waits=[f"{ut}pev"] + ([prev_sinB] if prev_sinB else []))
            E.op("dve", lambda v: v.tensor_scalar(S(1), S(0), MAGIC, MAGIC,
                                                  ALU.add, ALU.subtract))
            E.op("dve", lambda v: v.scalar_tensor_tensor(
                S(2), S(1), -TWO_PI, pv, ALU.mult, ALU.add), inc="s_dve")
            E.snap(f"{ut}rs", "s_dve")
            E.op("act", lambda s_: s_.activation(sin_dst, S(2), AF.Sin),
                 waits=[f"{ut}rs"] + list(sin_waits), inc="s_act")
            E.snap(f"{ut}sinA", "s_act")
            E.op("dve", lambda v: v.tensor_single_scalar(S(1), S(0), 0.25, ALU.add),
                 waits=[f"{ut}sinA"])
            E.op("dve", lambda v: v.tensor_scalar(S(0), S(1), MAGIC, MAGIC,
                                                  ALU.add, ALU.subtract))
            E.op("dve", lambda v: v.tensor_single_scalar(S(1), pv, PI / 2.0,
                                                         ALU.add))
            E.op("dve", lambda v: v.scalar_tensor_tensor(
                S(2), S(0), -TWO_PI, S(1), ALU.mult, ALU.add), inc="s_dve")
            E.snap(f"{ut}rc", "s_dve")
            E.op("act", lambda s_: s_.activation(cos_dst, S(2), AF.Sin),
                 waits=[f"{ut}rc"], inc="s_act")
            E.snap(f"{ut}sinB", "s_act")

        for s in range(4):
            st = f"B{s}"
            pvs = f"B{s-1}" if s > 0 else None
            s0 = s * 512
            E.op("sp", lambda sy, s0=s0: sy.dma_start(
                out=T["x1s_sb"][:].rearrange("p (c t) -> p c t", c=KC),
                in_=x13[:, :, s0:s0 + 512]),
                waits=([f"{pvs}ydone", f"{pvs}sqL"] if s > 0 else []),
                inc="s_xin", dma=True)
            E.snap(f"{st}xl", "s_xin")
            E.op("pool", lambda g, s0=s0: g.dma_start(
                out=T["cos_sl"][:], in_=D["cos_fm"][:, s0:s0 + 512]),
                waits=([f"{pvs}ropeL"] if s > 0 else []), inc="s_misc", dma=True)
            E.op("pool", lambda g, s0=s0: g.dma_start(
                out=T["sin_sl"][:], in_=D["sin_rh"][:, s0:s0 + 512]),
                inc="s_misc", dma=True)
            E.snap(f"{st}tab", "s_misc")
            xs_view = lambda c, ts: T["x1s_sb"][:, c * 512:(c + 1) * 512]
            y_view = lambda c, ts: T["y2"][:, c * 512:(c + 1) * 512]
            w0 = {"pe": [f"{st}xl"] + ([f"{pvs}fmaL"] if s > 0 else []),
                  "act": [f"{st}xl"],
                  "dve": [f"{st}xl"] + ([f"{pvs}u31proj"] if s > 0 else [])}
            em_ln(st, xs_view, y_view, 1, T, w0)
            E.snap(f"{st}sqL", "s_act")

            for i in range(NKV):
                proj_unit(st, pvs, s0, i, "k", i, rp=i, slot=i)
            for i in range(NKV):
                proj_unit(st, pvs, s0, NKV + i, "v", i, rp=None, slot=None)
            for kv in range(NKV):
                ft = f"{st}f{kv}"
                ex = []
                if kv >= 1:
                    ex.append(f"{st}f{kv-1}pev")
                elif s > 0:
                    ex.append(f"{pvs}q15pev")
                E.op("pe", lambda t, kv=kv: t.matmul(
                    banks[3][0:64, :], P["rT_bf"][:, 0:64],
                    T["krotbf"][:, kv * 512:(kv + 1) * 512],
                    start=True, stop=True),
                    waits=[f"{st}r{kv}xhat"] + ex, inc="s_pe")
                E.snap(f"{ft}pmm", "s_pe")
                sin_w = [f"{pvs}prodL"] if (s > 0 and kv == 0) else []
                prevB = (f"{st}f{kv-1}sinB" if kv >= 1 else
                         (f"{pvs}q15sinB" if s > 0 else None))
                prevrc = (f"{st}f{kv-2}rc" if kv >= 2 else
                          (f"{pvs}q{14+kv}rc" if s > 0 else None))
                range_sin(ft, kv % 2,
                          T["kp"][0:64, kv * 512:(kv + 1) * 512], sin_w,
                          T["kp"][64:128, kv * 512:(kv + 1) * 512], prevB,
                          prevrc)

            for grp in range(2):
                for j in range(8):
                    h = grp * 8 + j
                    proj_unit(st, pvs, s0, 16 + 8 * grp + j, "q", h,
                              rp=8 + h, slot=8 + j)
                for j in range(8):
                    h = grp * 8 + j
                    qt = f"{st}q{h}"
                    slot = 8 + j
                    ex = [f"{st}f{NKV-1}pev" if h == 0 else f"{st}q{h-1}pev"]
                    E.op("pe", lambda t, slot=slot: t.matmul(
                        banks[3][0:64, :], P["rT_bf"][:, 0:64],
                        T["krotbf"][:, slot * 512:(slot + 1) * 512],
                        start=True, stop=True),
                        waits=[f"{st}r{8+h}xhat"] + ex, inc="s_pe")
                    E.snap(f"{qt}pmm", "s_pe")
                    hp = h % 2
                    slot = 8 + j
                    if grp == 0:
                        qp_w = [f"{pvs}q{8+j}prodL"] if s > 0 else []
                    else:
                        qp_w = [f"{st}q{j}prodL"]
                    prevB = (f"{st}q{h-2}sinB" if h >= 2 else
                             f"{st}f{6+h}sinB")
                    prevrc = (f"{st}q{h-2}rc" if h >= 2 else
                              f"{st}f{NKV-2+h}rc")
                    qpv = T["krotbf"][:, slot * 512:(slot + 1) * 512]
                    range_sin(qt, h % 2,
                              qpv[0:64, :], qp_w, qpv[64:128, :], prevB,
                              prevrc)
                    for kv in range(NKV):
                        i = h * NKV + kv
                        par = i % 2
                        bnk = banks[4 + (i % 4)]
                        pw2 = [f"{qt}sinB", f"{st}f{NKV-1}sinB"]
                        if i >= 2:
                            pw2.append(f"{st}Bmm{i-2}")
                        elif s > 0:
                            pw2.append(f"{pvs}BmmL")
                        E.op("pool", lambda g, kv=kv, par=par, qpv=qpv: g.tensor_tensor(
                            T["prod"][:, par * 512:par * 512 + 512], qpv,
                            T["kp"][:, kv * 512:(kv + 1) * 512], ALU.mult),
                            waits=pw2, inc="s_castp")
                        E.snap(f"{st}prod{i}", "s_castp")
                        bw2 = [f"{st}prod{i}"]
                        if i >= 4:
                            bw2.append(f"{st}fma{i-4}")
                        else:
                            bw2.append(f"{st}ydone")
                            if s > 0:
                                bw2.append(f"{pvs}fmaL")
                        E.op("pe", lambda t, par=par, bnk=bnk: t.matmul(
                            bnk[:, :], P["ones128"][:],
                            T["prod"][:, par * 512:par * 512 + 512],
                            start=True, stop=True),
                            waits=bw2, inc="s_pe")
                        E.snap(f"{st}Bmm{i}", "s_pe")
                        fw = [f"{st}Bmm{i}"]
                        oav = T["outacc"][:, hp * 512:hp * 512 + 512]
                        if kv == 0 and h > 1:
                            fw.append(f"{st}zev{h-2}")
                        elif kv == 0 and s > 0:
                            fw.append(f"{pvs}zev{14 + h}")
                        if kv == 0:
                            E.op("dve", lambda v, kv=kv, bnk=bnk, oav=oav: v.tensor_mul(
                                oav,
                                T["v_sb"][:, kv * 512:(kv + 1) * 512],
                                bnk[:, :]),
                                waits=fw, inc="s_dve")
                        else:
                            E.op("dve", lambda v, kv=kv, bnk=bnk: v.tensor_mul(
                                T["dscr"][:],
                                T["v_sb"][:, kv * 512:(kv + 1) * 512],
                                bnk[:, :]), waits=fw)
                            E.op("dve", lambda v, oav=oav: v.tensor_add(
                                oav, oav, T["dscr"][:]),
                                inc="s_dve")
                        E.snap(f"{st}fma{i}", "s_dve")
                    E.snap(f"{st}fmah{h}", "s_dve")
                    E.op("act", lambda s_, h=h, s=s, hp=hp: s_.activation(
                        Z[:].rearrange("p (j t) -> p j t", j=16)
                         [:, :, 128 * h + 32 * s: 128 * h + 32 * s + 32],
                        T["outacc"][:, hp * 512:hp * 512 + 512]
                        .rearrange("p (sg j) -> p j sg", j=16),
                        AF.Copy, scale=1.0 / 64.0),
                        waits=[f"{st}fmah{h}"], inc="s_act")
                    E.snap(f"{st}zev{h}", "s_act")
                    E.snap(f"{st}q{h}prodL", "s_dve")
            E.snap(f"{st}ropeL", "s_dve")
            E.snap(f"{st}prodL", "s_dve")
            E.snap(f"{st}BmmL", "s_pe")
            E.snap(f"{st}fmaL", "s_dve")
            E.snap(f"{st}pmmL", "s_pe")
        E.snap("Bsl_act", "s_act")
        E.snap("Bsl_dve", "s_dve")
        E.snap("Bsl_pe", "s_pe")

    def alloc_attn_oproj(es):
        T = {}
        def sb(name, shape, dt):
            T[name] = es.enter_context(nc.sbuf_tensor("op_" + name, shape, dt))
        sb("wo_sb", [128, KC * DM], BF16)
        sb("x1o", [128, KC * 512], F32)
        sb("x2o", [128, KC * 512], F32)
        return T

    def sched_attn_oproj(T, Z, dst):
        E.op("pool", lambda g: g.dma_start(
            out=T["wo_sb"][:].rearrange("p (k n) -> p k n", k=KC),
            in_=D["wo_T"].rearrange("(k p) n -> p k n", p=128)),
            waits=["Bsl_act", "Bsl_dve", "Bsl_pe"],
            inc="s_misc", dma=True)
        E.snap("Bwo", "s_misc")
        x13 = D["x1s"].rearrange("(c p) t -> p c t", p=128)
        dst3 = dst.rearrange("(c p) t -> p c t", p=128)
        for ns in range(4):
            ot = f"Bo{ns}"
            lw = [("s_misc", E.snaps["Bwo"][1])]
            if ns > 0:
                lw.append(f"Bo{ns-1}x2L")
            E.op("sp", lambda sy, ns=ns: sy.dma_start(
                out=T["x1o"][:].rearrange("p (c t) -> p c t", c=KC),
                in_=x13[:, :, ns * 512:(ns + 1) * 512]),
                waits=lw, inc="s_xin", dma=True)
            E.snap(f"{ot}xl", "s_xin")
            for of in range(16):
                par = of % 2
                pw = [("s_misc", E.snaps["Bwo"][1]), "Bsl_act"]
                if of >= 2:
                    pw.append(f"{ot}x2_{of-2}")
                elif ns > 0:
                    pw.append(f"Bo{ns-1}x2_{14 + of}")
                for k in range(KC):
                    E.op("pe", lambda t, par=par, k=k, of=of, ns=ns: t.matmul(
                        banks[par][:, :],
                        T["wo_sb"][:, k * DM + of * 128: k * DM + (of + 1) * 128],
                        Z[:, k * DM + ns * 512: k * DM + ns * 512 + 512],
                        start=(k == 0), stop=(k == KC - 1)),
                        waits=(pw if k == 0 else []),
                        inc=("s_pe" if k == KC - 1 else None))
                E.snap(f"{ot}mm{of}", "s_pe")
                xw = [f"{ot}mm{of}", f"{ot}xl"]
                if ns > 0 and of == 0:
                    xw.append(f"Bo{ns-1}st")
                E.op("dve", lambda v, par=par, of=of: v.scalar_tensor_tensor(
                    T["x2o"][:, of * 512:(of + 1) * 512], banks[par][:, :],
                    P["bo"][:, of:of + 1], T["x1o"][:, of * 512:(of + 1) * 512],
                    ALU.add, ALU.add),
                    waits=xw, inc="s_dve")
                E.snap(f"{ot}x2_{of}", "s_dve")
            E.snap(f"{ot}x2L", "s_dve")
            E.op("sp", lambda sy, ns=ns: sy.dma_start(
                out=dst3[:, :, ns * 512:(ns + 1) * 512],
                in_=T["x2o"][:].rearrange("p (c t) -> p c t", c=KC)),
                waits=[f"{ot}x2L"], inc="s_st", dma=True)
            E.snap(f"{ot}st", "s_st")
        E.snap("Bend", "s_st")

    if "A" in phases:
        with ExitStack() as es:
            T = alloc_ffn(es, "fA_")
            dstA = D["x1s"] if ("B" in phases or "C" in phases) else D["out_fm"]
            with nc.Block("phaseA") as blk:
                run_block(blk, lambda: sched_ffn("A", T, D["x_fm"], dstA, []))

    if "B" in phases:
        with ExitStack() as es_z:
            Z = es_z.enter_context(nc.sbuf_tensor("Z", [128, KC * DM], BF16))
            with ExitStack() as es:
                T = alloc_attn_slices(es)
                with nc.Block("phaseB1") as blk:
                    run_block(blk, lambda: sched_attn_slices(T, Z))
            with ExitStack() as es:
                T = alloc_attn_oproj(es)
                dstB = D["x2s"] if "C" in phases else D["out_fm"]
                with nc.Block("phaseB2") as blk:
                    run_block(blk, lambda: sched_attn_oproj(T, Z, dstB))

    if "C" in phases:
        with ExitStack() as es:
            T = alloc_ffn(es, "fC_")
            srcC = D["x2s"] if "B" in phases else (
                D["x1s"] if "A" in phases else D["x_fm"])
            def schedC():
                start = []
                if "B" in phases:
                    start = [("s_st", E.snaps["Bend"][1])]
                elif "A" in phases:
                    start = [("s_st", E.snaps["Aend"][1])]
                sched_ffn("C", T, srcC, D["out_fm"], start)
            with nc.Block("phaseC") as blk:
                run_block(blk, schedC)

    def sched_final():
        E.wait("sp", [("s_st", E.cnt["s_st"])])
    with nc.Block("final") as blk:
        run_block(blk, sched_final)

    sem_es.close()
    pes.close()
    return nc


def kernel(**inputs):
    H = _host_prep(inputs)
    nc = build_nc()
    x = np.asarray(inputs["L0_attn_hidden_states"], np.float32)
    in_maps = []
    for c in range(8):
        m = dict(H)
        m["x_fm"] = np.ascontiguousarray(x[c].T)
        in_maps.append(m)
    res = run_bass_kernel_spmd(nc, in_maps, core_ids=list(range(8)))
    out = np.stack([np.asarray(res.results[c]["out_fm"]).T for c in range(8)])
    return out.astype(np.float32)


# version 11
# speedup vs baseline: 1.0400x; 1.0400x over previous
import numpy as np
import ml_dtypes
import concourse.bass as bass
import concourse.mybir as mybir
from concourse.bass_utils import run_bass_kernel_spmd
from contextlib import ExitStack

F32 = mybir.dt.float32
BF16 = mybir.dt.bfloat16
AF = mybir.ActivationFunctionType
ALU = mybir.AluOpType
AX = mybir.AxisListType

KC = 16
DM = 2048
NTOK = 2048
MACRO = 1024
NMACRO = 2
NE = 17
NKV = 8
NQH = 16
NPM = NE * 12
PI = float(np.pi)
MAGIC = 12582912.0
INV2PI = float(1.0 / (2.0 * np.pi))
TWO_PI = float(2.0 * np.pi)
LN_EPS = 1e-5

SEMS = ["s_pe", "s_act", "s_dve",
        "s_xin", "s_st", "s_misc", "s_cast", "s_castp",
        "s_wA", "s_wB", "s_wD", "s_w0", "s_w1"]


class Emit:

    def __init__(self, sems):
        self.sems = sems
        self.snaps = {}
        self.saved = {}

    def start(self, who, eng):
        self.who = who
        self.eng = eng
        self.cnt = dict(self.saved.get(who, {k: 0 for k in self.sems}))

    def finish(self):
        self.saved[self.who] = dict(self.cnt)

    def snap(self, key, sem):
        v = self.cnt[sem]
        old = self.snaps.get(key)
        if old is not None:
            assert old == (sem, v), f"snap {key}: {old} != {(sem, v)}"
        self.snaps[key] = (sem, v)

    def _waits(self, waits):
        for w in waits:
            if isinstance(w, str):
                s, v = self.snaps[w]
            else:
                s, v = w
            if v > 0:
                self.eng.wait_ge(self.sems[s], v)

    def op(self, engname, fn, waits=(), inc=None, n=1, dma=False):
        amt = 16 if dma else n
        if inc is not None:
            self.cnt[inc] += amt
        if self.who != engname:
            return
        self._waits(waits)
        ins = fn(self.eng)
        if inc is not None:
            ins.then_inc(self.sems[inc], amt)

    def wait(self, engname, waits):
        if self.who != engname:
            return
        self._waits(waits)



def _host_prep(inp):
    f32 = np.float32
    H = {}
    gup = np.asarray(inp["L0_ffn_gate_up_proj"], f32)
    dn = np.asarray(inp["L0_ffn_down_proj"], f32)
    shg = np.asarray(inp["L0_ffn_shared_gate_w"], f32)
    shu = np.asarray(inp["L0_ffn_shared_up_w"], f32)
    shd = np.asarray(inp["L0_ffn_shared_down_w"], f32)
    perm = []
    for j in range(6):
        perm += list(range(j * 128, (j + 1) * 128))
        perm += list(range(768 + j * 128, 768 + (j + 1) * 128))
    gup_T = np.empty((NE, DM, 1536), f32)
    for e in range(16):
        gup_T[e] = gup[e].T[:, perm]
    sh = np.empty((DM, 1536), f32)
    sgT, suT = shg.T, shu.T
    for j in range(6):
        sh[:, (2 * j) * 128:(2 * j + 1) * 128] = sgT[:, j * 128:(j + 1) * 128]
        sh[:, (2 * j + 1) * 128:(2 * j + 2) * 128] = suT[:, j * 128:(j + 1) * 128]
    gup_T[16] = sh
    H["gup_T"] = np.ascontiguousarray(gup_T)
    dn_T = np.empty((NE, 768, DM), f32)
    for e in range(16):
        dn_T[e] = dn[e].T
    dn_T[16] = shd.T
    H["dn_T"] = np.ascontiguousarray(dn_T)

    H["router_T"] = np.ascontiguousarray(np.asarray(inp["L0_ffn_router_w"], f32).T)
    H["wq_T"] = np.ascontiguousarray(np.asarray(inp["L0_attn_Wq"], f32).T)
    H["wk_T"] = np.ascontiguousarray(np.asarray(inp["L0_attn_Wk"], f32).T)
    H["wv_T"] = np.ascontiguousarray(np.asarray(inp["L0_attn_Wv"], f32).T)
    H["wo_T"] = np.ascontiguousarray(np.asarray(inp["L0_attn_Wo"], f32).T)
    H["r_T"] = np.ascontiguousarray(np.asarray(inp["L0_attn_rand_matrix"], f32).T)

    def fm_bias(b, nch):
        return np.ascontiguousarray(np.asarray(b, f32).reshape(nch, 128).T)
    H["bq_fm"] = fm_bias(inp["L0_attn_bq"], 16)
    H["bk_fm"] = fm_bias(inp["L0_attn_bk"], 8)
    H["bq_sh"] = np.ascontiguousarray(np.roll(H["bq_fm"], -64, axis=0))
    H["bk_sh"] = np.ascontiguousarray(np.roll(H["bk_fm"], -64, axis=0))
    H["bv_fm"] = fm_bias(inp["L0_attn_bv"], 8)
    H["bo_fm"] = fm_bias(inp["L0_attn_bo"], 16)
    H["ebias_t"] = np.ascontiguousarray(
        np.broadcast_to(np.asarray(inp["L0_ffn_e_bias"], f32), (128, 16)))

    H["ident"] = np.eye(128, dtype=f32)
    H["onesf"] = np.ones((128, 1), f32)
    H["onesf_row"] = np.ones((1, 128), f32)
    H["onesbf_col"] = np.ones((128, 1), f32)
    H["onesbf_row"] = np.ones((1, 128), f32)
    oh = np.zeros((16, 2048), f32)
    for e in range(16):
        oh[e, e * 128:(e + 1) * 128] = 1
    H["onehot"] = oh
    H["desc_t"] = np.ascontiguousarray(
        np.broadcast_to((16.0 - np.arange(16, dtype=f32)), (128, 16)))
    H["ones128"] = np.ones((128, 128), f32)
    H["eps12"] = np.full((1, 1), 1e-12, f32)
    H["ebias8"] = np.ascontiguousarray(np.tile(
        np.asarray(inp["L0_ffn_e_bias"], f32)[None, :], (128, 8)))
    H["desc8"] = np.ascontiguousarray(np.tile(
        (16.0 - np.arange(16, dtype=f32))[None, :], (128, 8)))

    invf = (1.0 / (10000.0 ** (np.arange(0, 128, 2, dtype=f32) / 128.0))).astype(f32)
    freqs = (np.arange(NTOK, dtype=f32)[:, None] * invf[None, :]).astype(f32)
    emb = np.concatenate([freqs, freqs], axis=1)
    H["cos_fm"] = np.ascontiguousarray(np.cos(emb).astype(f32).T)
    H["sin_fm"] = np.ascontiguousarray(np.sin(emb).astype(f32).T)
    H["sin_sh"] = np.ascontiguousarray(np.roll(H["sin_fm"], -64, axis=0))
    srh = H["sin_fm"].copy()
    srh[0:64] = -srh[0:64]
    H["sin_rh"] = np.ascontiguousarray(srh)
    return H



def build_nc(phases=("A", "B", "C"), dbg=False):
    nc = bass.Bass("TRN2", target_bir_lowering=False)
    D = {}
    D["x_fm"] = nc.dram_tensor("x_fm", [DM, NTOK], F32, kind="ExternalInput").ap()
    D["out_fm"] = nc.dram_tensor("out_fm", [DM, NTOK], F32, kind="ExternalOutput").ap()
    if dbg:
        D["dbg_y"] = nc.dram_tensor("dbg_y", [DM, NTOK], F32,
                                    kind="ExternalOutput").ap()
        D["dbg_comb"] = nc.dram_tensor("dbg_comb", [16, NTOK], F32,
                                       kind="ExternalOutput").ap()
        D["dbg_sc"] = nc.dram_tensor("dbg_sc", [128, 16 * 16], F32,
                                     kind="ExternalOutput").ap()
    D["x1s"] = nc.dram_tensor("x1s", [DM, NTOK], F32, kind="Internal").ap()
    D["x2s"] = nc.dram_tensor("x2s", [DM, NTOK], F32, kind="Internal").ap()
    for name, shape in [
        ("gup_T", [NE, DM, 1536]), ("dn_T", [NE, 768, DM]),
        ("router_T", [DM, 16]), ("wq_T", [DM, DM]), ("wk_T", [DM, 1024]),
        ("wv_T", [DM, 1024]), ("wo_T", [DM, DM]), ("r_T", [128, 64]),
        ("bq_fm", [128, 16]), ("bk_fm", [128, 8]), ("bv_fm", [128, 8]),
        ("bq_sh", [128, 16]), ("bk_sh", [128, 8]),
        ("bo_fm", [128, 16]), ("ebias_t", [128, 16]), ("ident", [128, 128]),
        ("onesf", [128, 1]), ("onesf_row", [1, 128]), ("desc_t", [128, 16]),
        ("ones128", [128, 128]), ("ebias8", [128, 128]), ("desc8", [128, 128]),
        ("eps12", [1, 1]),
        ("cos_fm", [128, NTOK]), ("sin_fm", [128, NTOK]), ("sin_rh", [128, NTOK]),
    ]:
        D[name] = nc.dram_tensor(name, shape, F32, kind="ExternalInput").ap()
    for name, shape in [
        ("onesbf_col", [128, 1]), ("onesbf_row", [1, 128]), ("onehot", [16, 2048]),
    ]:
        D[name] = nc.dram_tensor(name, shape, F32, kind="ExternalInput").ap()

    sem_es = ExitStack()
    sems = {s: sem_es.enter_context(nc.semaphore(s)) for s in SEMS}
    E = Emit(sems)

    pes = ExitStack()
    P = {}
    def psb(name, shape, dt):
        P[name] = pes.enter_context(nc.sbuf_tensor("sb_" + name, shape, dt))
    psb("ident", [128, 128], F32)
    psb("onesf", [128, 1], F32)
    psb("onesf_row", [1, 128], F32)
    psb("onesbf_col", [128, 1], BF16)
    psb("onesbf_row", [1, 128], BF16)
    psb("oh_bf", [16, 2048], BF16)
    psb("bq", [128, 16], F32)
    psb("bk", [128, 8], F32)
    psb("bq_sh", [128, 16], F32)
    psb("bk_sh", [128, 8], F32)
    psb("bv", [128, 8], F32)
    psb("bo", [128, 16], F32)
    psb("router_bf", [128, KC * 16], BF16)
    psb("rT_bf", [128, 64], BF16)
    psb("ones128", [128, 128], BF16)
    psb("ebias8", [128, 128], F32)
    psb("eps12", [1, 1], F32)
    psb("desc8", [128, 128], BF16)
    banks = [pes.enter_context(nc.psum_tensor(f"bank{i}", [128, 512], F32))
             for i in range(8)]

    def run_block(blk, schedule):
        engmap = [("sp", blk.sync), ("pool", blk.gpsimd), ("pe", blk.tensor),
                  ("act", blk.scalar), ("dve", blk.vector)]
        for who, starter in engmap:
            def runner(eng, who=who):
                E.start(who, eng)
                schedule()
                E.finish()
            starter(runner)

    def sched_prologue():
        pairs = [("ident", "ident"), ("onesf", "onesf"), ("onesf_row", "onesf_row"),
                 ("bq", "bq_fm"), ("bk", "bk_fm"), ("bv", "bv_fm"), ("bo", "bo_fm"),
                 ("bq_sh", "bq_sh"), ("bk_sh", "bk_sh"),
                 ("ebias8", "ebias8"), ("eps12", "eps12")]
        for dst, src in pairs:
            E.op("sp", lambda sy, d=dst, s=src: sy.dma_start(out=P[d][:], in_=D[s]),
                 inc="s_cast", dma=True)
        for dst, src in [("onesbf_col", "onesbf_col"), ("onesbf_row", "onesbf_row"),
                         ("oh_bf", "onehot"), ("ones128", "ones128"),
                         ("desc8", "desc8")]:
            E.op("pool", lambda g, d=dst, s=src: g.dma_start(out=P[d][:], in_=D[s]),
                 inc="s_castp", dma=True)
        E.op("pool", lambda g: g.dma_start(
            out=P["router_bf"][:].rearrange("p (k n) -> p k n", k=KC),
            in_=D["router_T"].rearrange("(k p) n -> p k n", p=128)),
            inc="s_castp", dma=True)
        E.op("pool", lambda g: g.dma_start(out=P["rT_bf"][:], in_=D["r_T"]),
             inc="s_castp", dma=True)
        E.snap("pro_sp", "s_cast")
        E.snap("pro_pool", "s_castp")

    with nc.Block("prologue") as blk:
        run_block(blk, sched_prologue)

    def em_ln(tag, xs_view, y_view, nts, T, waits0):
        rows, scr2 = T["rows"], T["scr2"]
        first = {k: list(v) for k, v in waits0.items()}
        def w0(eng):
            return first.pop(eng, [])
        sl = lambda i: rows[0:1, i * 512:(i + 1) * 512]
        for ts in range(nts):
            pe_extra = [f"{tag}y{ts-1}"] if ts > 0 else []
            for c in range(KC):
                E.op("act", lambda s, c=c, ts=ts: s.activation(
                    scr2[:, (c % 2) * 512:(c % 2) * 512 + 512], xs_view(c, ts),
                    AF.Square),
                    waits=w0("act") + ([f"{tag}sqm{ts}_{c-2}"] if c >= 2 else []),
                    inc="s_act")
                E.snap(f"{tag}sq{ts}_{c}", "s_act")
                E.op("pe", lambda t, c=c, ts=ts: t.matmul(
                    banks[6][0:1, :], P["onesf"][:], xs_view(c, ts),
                    start=(c == 0), stop=(c == KC - 1)),
                    waits=(w0("pe") + pe_extra) if c == 0 else [])
                E.op("pe", lambda t, c=c: t.matmul(
                    banks[7][0:1, :], P["onesf"][:],
                    scr2[:, (c % 2) * 512:(c % 2) * 512 + 512],
                    start=(c == 0), stop=(c == KC - 1)),
                    waits=[f"{tag}sq{ts}_{c}"], inc="s_pe")
                E.snap(f"{tag}sqm{ts}_{c}", "s_pe")
            E.snap(f"{tag}stat{ts}", "s_pe")
            E.op("dve", lambda v: v.tensor_scalar_mul(sl(0), banks[6][0:1, :], 1.0 / DM),
                 waits=w0("dve") + [f"{tag}stat{ts}"], inc="s_dve")
            E.op("dve", lambda v: v.tensor_scalar_mul(sl(1), banks[7][0:1, :], 1.0 / DM))
            E.op("dve", lambda v: v.tensor_mul(sl(2), sl(0), sl(0)))
            E.op("dve", lambda v: v.tensor_sub(sl(1), sl(1), sl(2)))
            E.op("dve", lambda v: v.tensor_single_scalar(sl(1), sl(1), LN_EPS, ALU.add),
                 inc="s_dve")
            E.snap(f"{tag}var{ts}", "s_dve")
            E.op("act", lambda s: s.activation(sl(2), sl(1), AF.Ln),
                 waits=[f"{tag}var{ts}"])
            E.op("act", lambda s: s.activation(sl(1), sl(2), AF.Exp, scale=-0.5),
                 inc="s_act")
            E.snap(f"{tag}rstd{ts}", "s_act")
            E.op("dve", lambda v: v.scalar_tensor_tensor(
                sl(0), sl(0), -1.0, sl(1), ALU.mult, ALU.mult),
                waits=[f"{tag}rstd{ts}"], inc="s_dve")
            E.snap(f"{tag}brow{ts}", "s_dve")
            E.op("pe", lambda t: t.matmul(banks[6][:, :], P["onesf_row"][:], sl(1),
                                          start=True, stop=True),
                 waits=[f"{tag}brow{ts}"])
            E.op("pe", lambda t: t.matmul(banks[7][:, :], P["onesf_row"][:], sl(0),
                                          start=True, stop=True), inc="s_pe")
            E.snap(f"{tag}bc{ts}", "s_pe")
            for c in range(KC):
                E.op("dve", lambda v, c=c, ts=ts: v.tensor_mul(
                    T["t1"][:], xs_view(c, ts), banks[6][:, :]),
                    waits=([f"{tag}bc{ts}"] if c == 0 else []))
                E.op("dve", lambda v, c=c, ts=ts: v.tensor_add(
                    y_view(c, ts), T["t1"][:], banks[7][:, :]), inc="s_dve")
            E.snap(f"{tag}y{ts}", "s_dve")
        E.snap(f"{tag}ydone", "s_dve")

    def alloc_ffn(es, pref):
        T = {}
        def sb(name, shape, dt):
            T[name] = es.enter_context(nc.sbuf_tensor(pref + name, shape, dt))
        sb("xs", [128, KC * MACRO], F32)
        sb("y", [128, KC * MACRO], BF16)
        sb("wA", [128, KC * 768], BF16)
        sb("wB", [128, KC * 768], BF16)
        sb("wD", [128, 6 * DM], BF16)
        sb("h", [128, 6 * MACRO], BF16)
        sb("c_sb", [128, 2 * MACRO], BF16)
        sb("scr2", [128, 2 * 512], F32)
        sb("t1", [128, 512], F32)
        sb("rows", [1, 3 * 512], F32)
        sb("scores", [128, 8 * 16], F32)
        sb("tk", [128, 3 * 128 + 32], F32)
        sb("combine_bf", [16, MACRO], BF16)
        return T

    def sched_ffn(tag, T, src, dst, start_waits):
        src3 = src.rearrange("(c p) t -> p c t", p=128)
        dst3 = dst.rearrange("(c p) t -> p c t", p=128)
        for eng in ("sp", "pool", "pe", "act", "dve"):
            E.wait(eng, list(start_waits) +
                   [("s_cast", E.snaps["pro_sp"][1]),
                    ("s_castp", E.snaps["pro_pool"][1])])
        for m in range(NMACRO):
            mt = f"{tag}{m}"
            pv = f"{tag}{m-1}"
            E.op("sp", lambda sy, m=m: sy.dma_start(
                out=T["xs"][:].rearrange("p (c t) -> p c t", c=KC),
                in_=src3[:, :, m * MACRO:(m + 1) * MACRO]),
                waits=([f"{pv}st"] if m > 0 else []),
                inc="s_xin", dma=True)
            E.snap(f"{mt}xl", "s_xin")

            xs_view = lambda c, ts: T["xs"][:, c * MACRO + ts * 512:
                                            c * MACRO + ts * 512 + 512]
            y_view = lambda c, ts: T["y"][:, c * MACRO + ts * 512:
                                          c * MACRO + ts * 512 + 512]
            w0 = {"pe": [f"{mt}xl"] + ([f"{pv}dnevL"] if m > 0 else []),
                  "act": [f"{mt}xl"] + ([f"{pv}sgfree"] if m > 0 else []),
                  "dve": [f"{mt}xl"] +
                         ([f"{pv}e{NE-1}guwBdone"] if m > 0 else [])}
            em_ln(mt, xs_view, y_view, 2, T, w0)

            for tc in range(8):
                bk = banks[6 + (tc % 2)]
                rw = [f"{mt}ydone"] + ([f"{mt}sig{tc-2}"] if tc >= 2 else [])
                for k in range(KC):
                    E.op("pe", lambda t, tc=tc, k=k, bk=bk: t.matmul(
                        bk[:, 0:16],
                        T["y"][:, k * MACRO + tc * 128: k * MACRO + tc * 128 + 128],
                        P["router_bf"][:, k * 16:(k + 1) * 16],
                        start=(k == 0), stop=(k == KC - 1)),
                        waits=(rw if k == 0 else []),
                        inc=("s_pe" if k == KC - 1 else None))
                E.snap(f"{mt}rt{tc}", "s_pe")
                E.op("act", lambda s, tc=tc, bk=bk: s.activation(
                    T["scores"][:, tc * 16:(tc + 1) * 16], bk[:, 0:16], AF.Sigmoid),
                    waits=[f"{mt}rt{tc}", f"{mt}ydone"], inc="s_act")
                E.snap(f"{mt}sig{tc}", "s_act")
            tk = T["tk"]
            seg = lambda ap: ap.rearrange("p (s e) -> p s e", e=16)
            bsc = tk[:, 0:128]; msk = tk[:, 128:256]; val = tk[:, 256:384]
            cw = tk[:, 0:128]; comb8 = tk[:, 256:384]
            m1 = tk[:, 384:392]; ws = tk[:, 392:400]; iv = tk[:, 400:408]
    
            def dsync(fn, waits=()):
                E.op("dve", fn,
                     waits=list(waits) + [("s_dve", E.cnt["s_dve"])],
                     inc="s_dve")
            bc = lambda ap: ap.unsqueeze(2).broadcast_to([128, 8, 16])
            tw = [f"{mt}sig7"]
            if m > 0:
                tw.append(f"{pv}ctpL")
            dsync(lambda v: v.tensor_add(bsc, T["scores"][:], P["ebias8"][:]),
                  waits=tw)
            dsync(lambda v: v.memset(msk, 0.0))
            for it in range(4):
                dsync(lambda v: v.reduce_max(m1, seg(bsc), axis=AX.X))
                dsync(lambda v: v.tensor_tensor(seg(val), seg(bsc), bc(m1),
                                                ALU.is_ge))
                dsync(lambda v: v.tensor_mul(val, val, P["desc8"][:]))
                dsync(lambda v: v.reduce_max(m1, seg(val), axis=AX.X))
                dsync(lambda v: v.tensor_tensor(seg(val), seg(val), bc(m1),
                                                ALU.is_equal))
                dsync(lambda v: v.tensor_add(msk, msk, val))
                dsync(lambda v: v.scalar_tensor_tensor(
                    bsc, val, -1e30, bsc, ALU.mult, ALU.add))
            dsync(lambda v: v.tensor_mul(cw, msk, T["scores"][:]))
            dsync(lambda v: v.reduce_sum(ws, seg(cw), axis=AX.X))
            dsync(lambda v: v.tensor_single_scalar(ws, ws, 1e-20, ALU.add))
            dsync(lambda v: v.reciprocal(iv, ws))
            dsync(lambda v: v.scalar_tensor_tensor(
                seg(comb8), seg(cw), 0.5, bc(iv), ALU.mult, ALU.mult))
            E.snap(f"{mt}comb", "s_dve")
            for tc in range(8):
                ctw = [f"{mt}comb"]
                if tc >= 1:
                    ctw.append(f"{mt}cbf{tc-1}")
                elif m > 0:
                    ctw += [f"{pv}silu{NPM-1}", f"{pv}hp{NPM-1}"]
                E.op("pe", lambda t, tc=tc: t.transpose(
                    banks[tc // 2][0:16, (tc % 2) * 128:(tc % 2) * 128 + 128],
                    comb8[:, tc * 16:(tc + 1) * 16], P["ident"][:]),
                    waits=ctw, inc="s_pe")
                E.snap(f"{mt}ctp{tc}", "s_pe")
                E.op("act", lambda s, tc=tc: s.activation(
                    T["combine_bf"][0:16, tc * 128:(tc + 1) * 128],
                    banks[tc // 2][0:16, (tc % 2) * 128:(tc % 2) * 128 + 128],
                    AF.Copy),
                    waits=[f"{mt}ctp{tc}"], inc="s_act")
                E.snap(f"{mt}cbf{tc}", "s_act")
            E.snap(f"{mt}ctpL", "s_pe")
            E.snap(f"{mt}cbfL", "s_act")
            if dbg and tag == "A":
                E.op("pool", lambda g, m=m: g.dma_start(
                    out=D["dbg_y"].rearrange("(c p) t -> p c t", p=128)
                    [:, :, m * MACRO:(m + 1) * MACRO],
                    in_=T["y"][:].rearrange("p (c t) -> p c t", c=KC)),
                    waits=[f"{mt}ydone"], inc="s_misc", dma=True)
                E.op("pool", lambda g, m=m: g.dma_start(
                    out=D["dbg_comb"][0:16, m * MACRO:(m + 1) * MACRO],
                    in_=T["combine_bf"][0:16, :]),
                    waits=[f"{mt}cbfL"], inc="s_misc", dma=True)
                E.op("pool", lambda g, m=m: g.dma_start(
                    out=D["dbg_sc"][:, m * 128:(m + 1) * 128],
                    in_=T["scores"][:]),
                    waits=[f"{mt}cbfL"], inc="s_misc", dma=True)
                E.snap(f"{mt}dbg", "s_misc")
                E.wait("sp", [f"{mt}dbg"])

            pair_ctr = 0
            for e in range(NE):
                et = f"{mt}e{e}"
                for role, semn, cols in (("wA", "s_wA", (0, 768)),
                                         ("wB", "s_wB", (768, 1536))):
                    kp = (f"{mt}e{e-1}gu{role}done" if e > 0 else
                          (f"{pv}e{NE-1}gu{role}done" if m > 0 else None))
                    E.op("pool", lambda g, e=e, cols=cols, role=role: g.dma_start(
                        out=T[role][:].rearrange("p (k n) -> p k n", k=KC),
                        in_=D["gup_T"][e, :, cols[0]:cols[1]].rearrange(
                            "(k p) n -> p k n", p=128)),
                        waits=([kp] if kp else []), inc=semn, dma=True)
                    E.snap(f"{et}{role}ld", semn)
                kp = (f"{mt}e{e-1}dnL" if e > 0 else
                      (f"{pv}e{NE-1}dnL" if m > 0 else None))
                E.op("pool", lambda g, e=e: g.dma_start(
                    out=T["wD"][:].rearrange("p (k n) -> p k n", k=6),
                    in_=D["dn_T"][e].rearrange("(k p) n -> p k n", p=128)),
                    waits=([kp] if kp else []), inc="s_wD", dma=True)
                E.snap(f"{et}wDld", "s_wD")

                if e < 16:
                    for ts in range(2):
                        if ts == 0:
                            cwaits = [f"{mt}cbfL"]
                            if e > 0:
                                cwaits.append(f"{mt}e{e-1}dnevL")
                            elif m > 0:
                                cwaits.append(f"{pv}dnevL")
                        else:
                            cwaits = [f"{et}cev0"]
                        E.op("pe", lambda t, e=e, ts=ts: t.matmul(
                            banks[6][:, :], P["oh_bf"][0:16, e * 128:(e + 1) * 128],
                            T["combine_bf"][0:16, ts * 512:(ts + 1) * 512],
                            start=True, stop=True),
                            waits=cwaits, inc="s_pe")
                        E.snap(f"{et}cmm{ts}", "s_pe")
                        hw = [f"{et}cmm{ts}"]
                        if e >= 2:
                            hw.append(f"{mt}e{e-2}hL")
                        elif m > 0:
                            hw.append(f"{pv}e{14 + e}hL")
                        E.op("act", lambda s, e=e, ts=ts: s.activation(
                            T["c_sb"][:, (e % 2) * MACRO + ts * 512:
                                      (e % 2) * MACRO + ts * 512 + 512],
                            banks[6][:, :], AF.Copy),
                            waits=hw, inc="s_act")
                        E.snap(f"{et}cev{ts}", "s_act")

                for gi, (hi, ts) in enumerate([(0, 0), (0, 1), (1, 0), (1, 1)]):
                    role = "wA" if hi == 0 else "wB"
                    gwaits = [f"{et}{role}ld", f"{mt}ydone"]
                    if pair_ctr >= 3:
                        gwaits += [f"{mt}silu{pair_ctr-1}", f"{mt}hp{pair_ctr-1}"]
                    elif m > 0:
                        gwaits += [f"{pv}silu{NPM-1}", f"{pv}hp{NPM-1}"]
                    if e == 0 and gi == 0:
                        gwaits.append(f"{mt}cbfL")
                    for ofp in range(6):
                        for k in range(KC):
                            E.op("pe", lambda t, role=role, ofp=ofp, k=k, ts=ts:
                                 t.matmul(
                                     banks[ofp][:, :],
                                     T[role][:, k * 768 + ofp * 128:
                                             k * 768 + (ofp + 1) * 128],
                                     T["y"][:, k * MACRO + ts * 512:
                                            k * MACRO + ts * 512 + 512],
                                     start=(k == 0), stop=(k == KC - 1)),
                                 waits=(gwaits if (ofp == 0 and k == 0) else []),
                                 inc=("s_pe" if k == KC - 1 else None))
                        E.snap(f"{et}gu{gi}_{ofp}", "s_pe")
                    if hi == 0 and ts == 1:
                        E.snap(f"{et}guwAdone", "s_pe")
                    if hi == 1 and ts == 1:
                        E.snap(f"{et}guwBdone", "s_pe")
                    for j in range(3):
                        pj = pair_ctr
                        pair_ctr += 1
                        swaits = [f"{et}gu{gi}_{2*j}"]
                        if pj >= 2:
                            swaits.append(f"{mt}hp{pj-2}")
                        E.op("act", lambda s, pj=pj, j=j: s.activation(
                            T["scr2"][:, (pj % 2) * 512:(pj % 2) * 512 + 512],
                            banks[2 * j][:, :], AF.Silu),
                            waits=swaits, inc="s_act")
                        E.snap(f"{mt}silu{pj}", "s_act")
                        hwaits = [f"{mt}silu{pj}", f"{et}gu{gi}_{2*j+1}"]
                        if e < 16:
                            hwaits.append(f"{et}cev1")
                            E.op("dve", lambda v, pj=pj, e=e, ts=ts: v.tensor_mul(
                                T["t1"][:],
                                T["scr2"][:, (pj % 2) * 512:(pj % 2) * 512 + 512],
                                T["c_sb"][:, (e % 2) * MACRO + ts * 512:
                                          (e % 2) * MACRO + ts * 512 + 512]),
                                waits=hwaits)
                        else:
                            E.op("dve", lambda v, pj=pj: v.tensor_scalar_mul(
                                T["t1"][:],
                                T["scr2"][:, (pj % 2) * 512:(pj % 2) * 512 + 512],
                                0.5),
                                waits=hwaits)
                        hcol = (hi * 3 + j) * MACRO + ts * 512
                        E.op("dve", lambda v, hcol=hcol, j=j: v.tensor_mul(
                            T["h"][:, hcol:hcol + 512], T["t1"][:],
                            banks[2 * j + 1][:, :]),
                            inc="s_dve")
                        E.snap(f"{mt}hp{pj}", "s_dve")
                E.snap(f"{mt}e{e}hL", "s_dve")

                for ts in range(2):
                    for of in range(16):
                        didx = e * 32 + ts * 16 + of
                        bk = banks[6 + (didx % 2)]
                        dwaits = [f"{et}wDld", f"{mt}e{e}hL"]
                        if ts == 0 and of == 0 and e < 16:
                            dwaits.append(f"{et}cev1")
                        if didx >= 2:
                            dwaits.append(f"{mt}dnev{didx-2}")
                        elif m > 0:
                            dwaits.append(f"{pv}dnevL")
                        for j in range(6):
                            E.op("pe", lambda t, bk=bk, j=j, of=of, ts=ts: t.matmul(
                                bk[:, :],
                                T["wD"][:, j * DM + of * 128: j * DM + (of + 1) * 128],
                                T["h"][:, j * MACRO + ts * 512:
                                       j * MACRO + ts * 512 + 512],
                                start=(j == 0), stop=(j == 5)),
                                waits=(dwaits if j == 0 else []),
                                inc=("s_pe" if j == 5 else None))
                        E.snap(f"{mt}dnb{didx}", "s_pe")
                        E.op("dve", lambda v, bk=bk, of=of, ts=ts: v.tensor_add(
                            T["xs"][:, of * MACRO + ts * 512:
                                    of * MACRO + ts * 512 + 512],
                            T["xs"][:, of * MACRO + ts * 512:
                                    of * MACRO + ts * 512 + 512],
                            bk[:, :]),
                            waits=[f"{mt}dnb{didx}"], inc="s_dve")
                        E.snap(f"{mt}dnev{didx}", "s_dve")
                E.snap(f"{et}dnL", "s_pe")
                E.snap(f"{mt}e{e}dnevL", "s_dve")
            E.snap(f"{mt}dnevL", "s_dve")
            E.snap(f"{mt}sgfree", "s_dve")
            E.op("sp", lambda sy, m=m: sy.dma_start(
                out=dst3[:, :, m * MACRO:(m + 1) * MACRO],
                in_=T["xs"][:].rearrange("p (c t) -> p c t", c=KC)),
                waits=[f"{mt}dnevL"], inc="s_st", dma=True)
            E.snap(f"{mt}st", "s_st")
        E.snap(f"{tag}end", "s_st")

    def alloc_attn_slices(es):
        T = {}
        def sb(name, shape, dt):
            T[name] = es.enter_context(nc.sbuf_tensor("at_" + name, shape, dt))
        sb("x1s_sb", [128, KC * 512], F32)
        sb("y2", [128, KC * 512], BF16)
        sb("kp", [128, NKV * 512], BF16)
        sb("v_sb", [128, NKV * 512], BF16)
        sb("krotbf", [128, 16 * 512], BF16)
        sb("wu0", [128, KC * 128], BF16)
        sb("wu1", [128, KC * 128], BF16)
        sb("dscr", [128, 512], F32)
        sb("krot", [128, 2 * 512], BF16)
        sb("scrB", [128, 512], F32)
        sb("sq2", [128, 512], F32)
        sb("brow", [1, 3 * 512], F32)
        sb("rngs", [64, 6 * 512], F32)
        sb("p_sb", [64, 2 * 512], BF16)
        sb("prod", [128, 2 * 512], BF16)
        sb("outacc", [128, 2 * 512], F32)
        sb("scr2", [128, 2 * 512], F32)
        sb("rows", [1, 3 * 512], F32)
        sb("cos_sl", [128, 512], BF16)
        sb("sin_sl", [128, 512], BF16)
        T["t1"] = T["dscr"]
        return T

    def sched_attn_slices(T, Z):
        sw = [("s_st", E.snaps["Aend"][1])] if "Aend" in E.snaps else []
        for eng in ("sp", "pool", "pe", "act", "dve"):
            E.wait(eng, sw + [("s_cast", E.snaps["pro_sp"][1]),
                              ("s_castp", E.snaps["pro_pool"][1])])
        x13 = D["x1s"].rearrange("(c p) t -> p c t", p=128)

        def proj_part(st, pvs, pu, kind, idx):
            ut = f"{st}u{pu}"
            par = pu % 2
            wdram = {"k": "wk_T", "v": "wv_T", "q": "wq_T"}[kind]
            wsem = "s_w0" if par == 0 else "s_w1"
            lw = []
            if pu >= 2:
                lw.append(f"{st}u{pu-2}proj")
            elif pvs is not None:
                lw.append(f"{pvs}u{30+pu}proj")
            E.op("pool", lambda g: g.dma_start(
                out=T[f"wu{par}"][:].rearrange("p (k n) -> p k n", k=KC),
                in_=D[wdram][:, idx * 128:(idx + 1) * 128].rearrange(
                    "(k p) n -> p k n", p=128)),
                waits=lw, inc=wsem, dma=True)
            E.snap(f"{ut}wl", wsem)
            pw = [f"{ut}wl", f"{st}ydone"]
            if pu >= 2:
                pw.append(f"{st}u{pu-2}c")
            elif pvs is not None:
                pw.append(f"{pvs}u{30+pu}c")
            for k in range(KC):
                E.op("pe", lambda t, k=k: t.matmul(
                    banks[par][:, :],
                    T[f"wu{par}"][:, k * 128:(k + 1) * 128],
                    T["y2"][:, k * 512:(k + 1) * 512],
                    start=(k == 0), stop=(k == KC - 1)),
                    waits=(pw if k == 0 else []),
                    inc=("s_pe" if k == KC - 1 else None))
            E.snap(f"{ut}proj", "s_pe")

        def proj_unit(st, pvs, s0, pu, kind, idx, rp, slot):
            ut = f"{st}u{pu}"
            par = pu % 2
            if kind == "v":
                vw = [f"{ut}proj"]
                if pvs is not None:
                    vw.append(f"{pvs}fmaL")
                E.op("act", lambda s_: s_.activation(
                    T["v_sb"][:, idx * 512:(idx + 1) * 512], banks[par][:, :],
                    AF.Identity, bias=P["bv"][:, idx:idx + 1]),
                    waits=vw, inc="s_act")
                E.snap(f"{ut}c", "s_act")
                return
            bt = P["bk"] if kind == "k" else P["bq"]
            bsh = P["bk_sh"] if kind == "k" else P["bq_sh"]
            bias = bt[:, idx:idx + 1]
            assert rp is not None
            rt = f"{st}r{rp}"
            rpar = rp % 2
            krv = T["krot"][:, rpar * 512:rpar * 512 + 512]
            cs, sn = T["cos_sl"][:], T["sin_sl"][:]
            rw = [f"{ut}proj", f"{st}tab"]
            if rp >= 2:
                rw.append(f"{st}r{rp-2}sq")
            E.op("dve", lambda v: v.scalar_tensor_tensor(
                T["dscr"][:], banks[par][:, :], bias, cs, ALU.add, ALU.mult),
                waits=rw)
            E.op("dve", lambda v: v.scalar_tensor_tensor(
                T["scrB"][0:64, :], banks[par][64:128, :], bsh[0:64, idx:idx + 1],
                sn[0:64, :], ALU.add, ALU.mult))
            E.op("dve", lambda v: v.scalar_tensor_tensor(
                T["scrB"][64:128, :], banks[par][0:64, :], bsh[64:128, idx:idx + 1],
                sn[64:128, :], ALU.add, ALU.mult))
            E.op("dve", lambda v: v.tensor_add(krv, T["dscr"][:], T["scrB"][:]),
                inc="s_dve")
            E.snap(f"{rt}rope", "s_dve")
            E.snap(f"{ut}c", "s_dve")
            sw2 = []
            if rp >= 1:
                sw2.append(f"{st}r{rp-1}nmm")
            E.op("dve", lambda v: v.tensor_mul(T["sq2"][:], krv, krv),
                 waits=sw2, inc="s_dve")
            E.snap(f"{rt}sq", "s_dve")
            nbi = 2 if (rp >= 16 or rp % 2 == 0) else 4
            nb = banks[nbi]
            nw = [f"{rt}sq"]
            prev_same = rp - 1 if rp >= 16 else rp - 2
            if prev_same >= 0:
                nw.append(f"{st}r{prev_same}xhat")
            if nbi == 4 and rp < 2 and pvs is not None:
                nw.append(f"{pvs}fmaL")
            E.op("pe", lambda t, nb=nb: t.matmul(
                nb[0:1, :], P["onesf"][:], T["sq2"][:],
                start=True, stop=True),
                waits=nw, inc="s_pe")
            E.snap(f"{rt}nmm", "s_pe")
            lint = T["brow"][0:1, 0:512]
            brw = T["brow"][0:1, (1 + rp % 2) * 512:(1 + rp % 2) * 512 + 512]
            bw = [f"{rt}nmm"]
            if rp >= 2:
                bw.append(f"{st}r{rp-2}ibc")
            E.op("act", lambda s_, nb=nb, lint=lint: s_.activation(
                lint, nb[0:1, :], AF.Ln, bias=P["eps12"][0:1, 0:1]),
                waits=bw)
            E.op("act", lambda s_, brw=brw, lint=lint: s_.activation(
                brw, lint, AF.Exp, scale=-0.5), inc="s_act")
            E.snap(f"{rt}inv", "s_act")
            E.op("pe", lambda t, nb=nb, brw=brw: t.matmul(
                nb[:, :], P["onesf_row"][:], brw,
                start=True, stop=True),
                waits=[f"{rt}inv"], inc="s_pe")
            E.snap(f"{rt}ibc", "s_pe")
            xw = [f"{rt}ibc"]
            if slot < 8:
                if pvs is not None and rp < 2:
                    xw.append(f"{pvs}pmmL")
            else:
                if rp >= 16:
                    xw.append(f"{st}q{slot-8}prodL")
                elif rp >= 8 and pvs is not None:
                    xw.append(f"{pvs}q{slot}prodL")
            E.op("dve", lambda v, nb=nb: v.tensor_mul(
                T["krotbf"][:, slot * 512:(slot + 1) * 512], krv, nb[:, :]),
                waits=xw, inc="s_dve")
            E.snap(f"{rt}xhat", "s_dve")

        def pev_part(ut, fpar, prev_rc):
            pv = T["p_sb"][:, fpar * 512:(fpar + 1) * 512]
            ev = [f"{ut}pmm"] + ([prev_rc] if prev_rc else [])
            E.op("act", lambda s_, pv=pv: s_.activation(
                pv, banks[3][0:64, :], AF.Copy),
                waits=ev, inc="s_act")
            E.snap(f"{ut}pev", "s_act")

        def range_sin(ut, fpar, sin_dst, sin_waits, cos_dst, prev_sinB):
            pv = T["p_sb"][:, fpar * 512:(fpar + 1) * 512]
            S = lambda i: T["rngs"][:, (fpar * 3 + i) * 512:
                                    (fpar * 3 + i + 1) * 512]
            E.op("dve", lambda v: v.tensor_scalar_mul(S(0), pv, INV2PI),
                 waits=[f"{ut}pev"] + ([prev_sinB] if prev_sinB else []))
            E.op("dve", lambda v: v.tensor_scalar(S(1), S(0), MAGIC, MAGIC,
                                                  ALU.add, ALU.subtract))
            E.op("dve", lambda v: v.scalar_tensor_tensor(
                S(2), S(1), -TWO_PI, pv, ALU.mult, ALU.add), inc="s_dve")
            E.snap(f"{ut}rs", "s_dve")
            E.op("act", lambda s_: s_.activation(sin_dst, S(2), AF.Sin),
                 waits=[f"{ut}rs"] + list(sin_waits), inc="s_act")
            E.snap(f"{ut}sinA", "s_act")
            E.op("dve", lambda v: v.tensor_single_scalar(S(1), S(0), 0.25, ALU.add),
                 waits=[f"{ut}sinA"])
            E.op("dve", lambda v: v.tensor_scalar(S(0), S(1), MAGIC, MAGIC,
                                                  ALU.add, ALU.subtract))
            E.op("dve", lambda v: v.tensor_single_scalar(S(1), pv, PI / 2.0,
                                                         ALU.add))
            E.op("dve", lambda v: v.scalar_tensor_tensor(
                S(2), S(0), -TWO_PI, S(1), ALU.mult, ALU.add), inc="s_dve")
            E.snap(f"{ut}rc", "s_dve")
            E.op("act", lambda s_: s_.activation(cos_dst, S(2), AF.Sin),
                 waits=[f"{ut}rc"], inc="s_act")
            E.snap(f"{ut}sinB", "s_act")

        for s in range(4):
            st = f"B{s}"
            pvs = f"B{s-1}" if s > 0 else None
            s0 = s * 512
            E.op("sp", lambda sy, s0=s0: sy.dma_start(
                out=T["x1s_sb"][:].rearrange("p (c t) -> p c t", c=KC),
                in_=x13[:, :, s0:s0 + 512]),
                waits=([f"{pvs}ydone", f"{pvs}sqL"] if s > 0 else []),
                inc="s_xin", dma=True)
            E.snap(f"{st}xl", "s_xin")
            E.op("pool", lambda g, s0=s0: g.dma_start(
                out=T["cos_sl"][:], in_=D["cos_fm"][:, s0:s0 + 512]),
                waits=([f"{pvs}ropeL"] if s > 0 else []), inc="s_misc", dma=True)
            E.op("pool", lambda g, s0=s0: g.dma_start(
                out=T["sin_sl"][:], in_=D["sin_rh"][:, s0:s0 + 512]),
                inc="s_misc", dma=True)
            E.snap(f"{st}tab", "s_misc")
            xs_view = lambda c, ts: T["x1s_sb"][:, c * 512:(c + 1) * 512]
            y_view = lambda c, ts: T["y2"][:, c * 512:(c + 1) * 512]
            w0 = {"pe": [f"{st}xl"] + ([f"{pvs}fmaL"] if s > 0 else []),
                  "act": [f"{st}xl"],
                  "dve": [f"{st}xl"] + ([f"{pvs}u31proj"] if s > 0 else [])}
            em_ln(st, xs_view, y_view, 1, T, w0)
            E.snap(f"{st}sqL", "s_act")

            kv_units = ([(i, "k", i, i, i) for i in range(NKV)] +
                        [(NKV + i, "v", i, None, None) for i in range(NKV)])
            proj_part(st, pvs, kv_units[0][0], kv_units[0][1], kv_units[0][2])
            for n_, (pu, kind, idx, rp, slot) in enumerate(kv_units):
                if n_ + 1 < len(kv_units):
                    nx = kv_units[n_ + 1]
                    proj_part(st, pvs, nx[0], nx[1], nx[2])
                proj_unit(st, pvs, s0, pu, kind, idx, rp=rp, slot=slot)
            for kv in range(NKV):
                ft = f"{st}f{kv}"
                ex = []
                if kv >= 1:
                    ex.append(f"{st}f{kv-1}pev")
                elif s > 0:
                    ex.append(f"{pvs}q15pev")
                E.op("pe", lambda t, kv=kv: t.matmul(
                    banks[3][0:64, :], P["rT_bf"][:, 0:64],
                    T["krotbf"][:, kv * 512:(kv + 1) * 512],
                    start=True, stop=True),
                    waits=[f"{st}r{kv}xhat"] + ex, inc="s_pe")
                E.snap(f"{ft}pmm", "s_pe")
                sin_w = [f"{pvs}prodL"] if (s > 0 and kv == 0) else []
                prevB = (f"{st}f{kv-1}sinB" if kv >= 1 else
                         (f"{pvs}q15sinB" if s > 0 else None))
                prevrc = (f"{st}f{kv-2}rc" if kv >= 2 else
                          (f"{pvs}q{14+kv}rc" if s > 0 else None))
                pev_part(ft, kv % 2, prevrc)
                range_sin(ft, kv % 2,
                          T["kp"][0:64, kv * 512:(kv + 1) * 512], sin_w,
                          T["kp"][64:128, kv * 512:(kv + 1) * 512], prevB)

            for grp in range(2):
                qunits = [(16 + 8 * grp + j, grp * 8 + j) for j in range(8)]
                proj_part(st, pvs, qunits[0][0], "q", qunits[0][1])
                for n_, (pu, h_) in enumerate(qunits):
                    if n_ + 1 < len(qunits):
                        proj_part(st, pvs, qunits[n_ + 1][0], "q",
                                  qunits[n_ + 1][1])
                    proj_unit(st, pvs, s0, pu, "q", h_,
                              rp=8 + h_, slot=8 + (h_ % 8))
                def q_pmm(h):
                    qt2 = f"{st}q{h}"
                    slot2 = 8 + (h % 8)
                    ex = [f"{st}f{NKV-1}pev" if h == 0 else f"{st}q{h-1}pev"]
                    E.op("pe", lambda t, slot2=slot2: t.matmul(
                        banks[3][0:64, :], P["rT_bf"][:, 0:64],
                        T["krotbf"][:, slot2 * 512:(slot2 + 1) * 512],
                        start=True, stop=True),
                        waits=[f"{st}r{8+h}xhat"] + ex, inc="s_pe")
                    E.snap(f"{qt2}pmm", "s_pe")
                    prevrc = (f"{st}q{h-2}rc" if h >= 2 else
                              f"{st}f{NKV-2+h}rc")
                    pev_part(qt2, h % 2, prevrc)

                q_pmm(grp * 8)
                for j in range(8):
                    h = grp * 8 + j
                    qt = f"{st}q{h}"
                    slot = 8 + j
                    hp = h % 2
                    if j + 1 < 8:
                        q_pmm(h + 1)
                    if grp == 0:
                        qp_w = [f"{pvs}q{8+j}prodL"] if s > 0 else []
                    else:
                        qp_w = [f"{st}q{j}prodL"]
                    prevB = (f"{st}q{h-2}sinB" if h >= 2 else
                             f"{st}f{6+h}sinB")
                    qpv = T["krotbf"][:, slot * 512:(slot + 1) * 512]
                    range_sin(qt, h % 2,
                              qpv[0:64, :], qp_w, qpv[64:128, :], prevB)

                    def emit_zev(hz):
                        E.op("act", lambda s_, hz=hz, s=s: s_.activation(
                            Z[:].rearrange("p (jj t) -> p jj t", jj=16)
                             [:, :, 128 * hz + 32 * s: 128 * hz + 32 * s + 32],
                            T["outacc"][:, (hz % 2) * 512:(hz % 2) * 512 + 512]
                            .rearrange("p (sg jj) -> p jj sg", jj=16),
                            AF.Copy, scale=1.0 / 64.0),
                            waits=[f"{st}fmah{hz}"], inc="s_act")
                        E.snap(f"{st}zev{hz}", "s_act")
                    if h >= 1:
                        emit_zev(h - 1)
                    for kv in range(NKV):
                        i = h * NKV + kv
                        par = i % 2
                        bnk = banks[4 + (i % 4)]
                        pw2 = [f"{qt}sinB", f"{st}f{NKV-1}sinB"]
                        if i >= 2:
                            pw2.append(f"{st}Bmm{i-2}")
                        elif s > 0:
                            pw2.append(f"{pvs}BmmL")
                        E.op("pool", lambda g, kv=kv, par=par, qpv=qpv: g.tensor_tensor(
                            T["prod"][:, par * 512:par * 512 + 512], qpv,
                            T["kp"][:, kv * 512:(kv + 1) * 512], ALU.mult),
                            waits=pw2, inc="s_castp")
                        E.snap(f"{st}prod{i}", "s_castp")
                        bw2 = [f"{st}prod{i}"]
                        if i >= 4:
                            bw2.append(f"{st}fma{i-4}")
                        else:
                            bw2.append(f"{st}ydone")
                            if s > 0:
                                bw2.append(f"{pvs}fmaL")
                        E.op("pe", lambda t, par=par, bnk=bnk: t.matmul(
                            bnk[:, :], P["ones128"][:],
                            T["prod"][:, par * 512:par * 512 + 512],
                            start=True, stop=True),
                            waits=bw2, inc="s_pe")
                        E.snap(f"{st}Bmm{i}", "s_pe")
                        fw = [f"{st}Bmm{i}"]
                        oav = T["outacc"][:, hp * 512:hp * 512 + 512]
                        if kv == 0 and h > 1:
                            fw.append(f"{st}zev{h-2}")
                        elif kv == 0 and s > 0:
                            fw.append(f"{pvs}zev{14 + h}")
                        if kv == 0:
                            E.op("dve", lambda v, kv=kv, bnk=bnk, oav=oav: v.tensor_mul(
                                oav,
                                T["v_sb"][:, kv * 512:(kv + 1) * 512],
                                bnk[:, :]),
                                waits=fw, inc="s_dve")
                        else:
                            E.op("dve", lambda v, kv=kv, bnk=bnk: v.tensor_mul(
                                T["dscr"][:],
                                T["v_sb"][:, kv * 512:(kv + 1) * 512],
                                bnk[:, :]), waits=fw)
                            E.op("dve", lambda v, oav=oav: v.tensor_add(
                                oav, oav, T["dscr"][:]),
                                inc="s_dve")
                        E.snap(f"{st}fma{i}", "s_dve")
                    E.snap(f"{st}fmah{h}", "s_dve")
                    E.snap(f"{st}q{h}prodL", "s_dve")
                    if h == 15:
                        emit_zev(15)
            E.snap(f"{st}ropeL", "s_dve")
            E.snap(f"{st}prodL", "s_dve")
            E.snap(f"{st}BmmL", "s_pe")
            E.snap(f"{st}fmaL", "s_dve")
            E.snap(f"{st}pmmL", "s_pe")
        E.snap("Bsl_act", "s_act")
        E.snap("Bsl_dve", "s_dve")
        E.snap("Bsl_pe", "s_pe")

    def alloc_attn_oproj(es):
        T = {}
        def sb(name, shape, dt):
            T[name] = es.enter_context(nc.sbuf_tensor("op_" + name, shape, dt))
        sb("wo_sb", [128, KC * DM], BF16)
        sb("x1o", [128, KC * 512], F32)
        sb("x2o", [128, KC * 512], F32)
        return T

    def sched_attn_oproj(T, Z, dst):
        E.op("pool", lambda g: g.dma_start(
            out=T["wo_sb"][:].rearrange("p (k n) -> p k n", k=KC),
            in_=D["wo_T"].rearrange("(k p) n -> p k n", p=128)),
            waits=["Bsl_act", "Bsl_dve", "Bsl_pe"],
            inc="s_misc", dma=True)
        E.snap("Bwo", "s_misc")
        x13 = D["x1s"].rearrange("(c p) t -> p c t", p=128)
        dst3 = dst.rearrange("(c p) t -> p c t", p=128)
        for ns in range(4):
            ot = f"Bo{ns}"
            lw = [("s_misc", E.snaps["Bwo"][1])]
            if ns > 0:
                lw.append(f"Bo{ns-1}x2L")
            E.op("sp", lambda sy, ns=ns: sy.dma_start(
                out=T["x1o"][:].rearrange("p (c t) -> p c t", c=KC),
                in_=x13[:, :, ns * 512:(ns + 1) * 512]),
                waits=lw, inc="s_xin", dma=True)
            E.snap(f"{ot}xl", "s_xin")
            for of in range(16):
                par = of % 2
                pw = [("s_misc", E.snaps["Bwo"][1]), "Bsl_act"]
                if of >= 2:
                    pw.append(f"{ot}x2_{of-2}")
                elif ns > 0:
                    pw.append(f"Bo{ns-1}x2_{14 + of}")
                for k in range(KC):
                    E.op("pe", lambda t, par=par, k=k, of=of, ns=ns: t.matmul(
                        banks[par][:, :],
                        T["wo_sb"][:, k * DM + of * 128: k * DM + (of + 1) * 128],
                        Z[:, k * DM + ns * 512: k * DM + ns * 512 + 512],
                        start=(k == 0), stop=(k == KC - 1)),
                        waits=(pw if k == 0 else []),
                        inc=("s_pe" if k == KC - 1 else None))
                E.snap(f"{ot}mm{of}", "s_pe")
                xw = [f"{ot}mm{of}", f"{ot}xl"]
                if ns > 0 and of == 0:
                    xw.append(f"Bo{ns-1}st")
                E.op("dve", lambda v, par=par, of=of: v.scalar_tensor_tensor(
                    T["x2o"][:, of * 512:(of + 1) * 512], banks[par][:, :],
                    P["bo"][:, of:of + 1], T["x1o"][:, of * 512:(of + 1) * 512],
                    ALU.add, ALU.add),
                    waits=xw, inc="s_dve")
                E.snap(f"{ot}x2_{of}", "s_dve")
            E.snap(f"{ot}x2L", "s_dve")
            E.op("sp", lambda sy, ns=ns: sy.dma_start(
                out=dst3[:, :, ns * 512:(ns + 1) * 512],
                in_=T["x2o"][:].rearrange("p (c t) -> p c t", c=KC)),
                waits=[f"{ot}x2L"], inc="s_st", dma=True)
            E.snap(f"{ot}st", "s_st")
        E.snap("Bend", "s_st")

    if "A" in phases:
        with ExitStack() as es:
            T = alloc_ffn(es, "fA_")
            dstA = D["x1s"] if ("B" in phases or "C" in phases) else D["out_fm"]
            with nc.Block("phaseA") as blk:
                run_block(blk, lambda: sched_ffn("A", T, D["x_fm"], dstA, []))

    if "B" in phases:
        with ExitStack() as es_z:
            Z = es_z.enter_context(nc.sbuf_tensor("Z", [128, KC * DM], BF16))
            with ExitStack() as es:
                T = alloc_attn_slices(es)
                with nc.Block("phaseB1") as blk:
                    run_block(blk, lambda: sched_attn_slices(T, Z))
            with ExitStack() as es:
                T = alloc_attn_oproj(es)
                dstB = D["x2s"] if "C" in phases else D["out_fm"]
                with nc.Block("phaseB2") as blk:
                    run_block(blk, lambda: sched_attn_oproj(T, Z, dstB))

    if "C" in phases:
        with ExitStack() as es:
            T = alloc_ffn(es, "fC_")
            srcC = D["x2s"] if "B" in phases else (
                D["x1s"] if "A" in phases else D["x_fm"])
            def schedC():
                start = []
                if "B" in phases:
                    start = [("s_st", E.snaps["Bend"][1])]
                elif "A" in phases:
                    start = [("s_st", E.snaps["Aend"][1])]
                sched_ffn("C", T, srcC, D["out_fm"], start)
            with nc.Block("phaseC") as blk:
                run_block(blk, schedC)

    def sched_final():
        E.wait("sp", [("s_st", E.cnt["s_st"])])
    with nc.Block("final") as blk:
        run_block(blk, sched_final)

    sem_es.close()
    pes.close()
    return nc


def kernel(**inputs):
    H = _host_prep(inputs)
    nc = build_nc()
    x = np.asarray(inputs["L0_attn_hidden_states"], np.float32)
    in_maps = []
    for c in range(8):
        m = dict(H)
        m["x_fm"] = np.ascontiguousarray(x[c].T)
        in_maps.append(m)
    res = run_bass_kernel_spmd(nc, in_maps, core_ids=list(range(8)))
    out = np.stack([np.asarray(res.results[c]["out_fm"]).T for c in range(8)])
    return out.astype(np.float32)


# version 14
# speedup vs baseline: 1.0835x; 1.0418x over previous
import numpy as np
import ml_dtypes
import concourse.bass as bass
import concourse.mybir as mybir
from concourse.bass_utils import run_bass_kernel_spmd
from contextlib import ExitStack

F32 = mybir.dt.float32
BF16 = mybir.dt.bfloat16
AF = mybir.ActivationFunctionType
ALU = mybir.AluOpType
AX = mybir.AxisListType

KC = 16
DM = 2048
NTOK = 2048
MACRO = 1024
NMACRO = 2
NE = 17
NKV = 8
NQH = 16
NPM = NE * 12
PI = float(np.pi)
MAGIC = 12582912.0
INV2PI = float(1.0 / (2.0 * np.pi))
TWO_PI = float(2.0 * np.pi)
LN_EPS = 1e-5

SEMS = ["s_pe", "s_act", "s_dve",
        "s_xin", "s_st", "s_misc", "s_cast", "s_castp",
        "s_wA", "s_wB", "s_wD", "s_w0", "s_w1",
        "s_r0", "s_r1", "s_sv0", "s_sv1", "s_wo0", "s_wo1",
        "s_o0", "s_o1", "s_x20", "s_x21"]
WRING = ["s_wA", "s_wB", "s_wD", "s_w0", "s_w1"]


class Emit:

    def __init__(self, sems):
        self.sems = sems
        self.snaps = {}
        self.saved = {}

    def start(self, who, eng):
        self.who = who
        self.eng = eng
        self.cnt = dict(self.saved.get(who, {k: 0 for k in self.sems}))

    def finish(self):
        self.saved[self.who] = dict(self.cnt)

    def snap(self, key, sem):
        v = self.cnt[sem]
        old = self.snaps.get(key)
        if old is not None:
            assert old == (sem, v), f"snap {key}: {old} != {(sem, v)}"
        self.snaps[key] = (sem, v)

    def _waits(self, waits):
        for w in waits:
            if isinstance(w, str):
                s, v = self.snaps[w]
            else:
                s, v = w
            if v > 0:
                self.eng.wait_ge(self.sems[s], v)

    def op(self, engname, fn, waits=(), inc=None, n=1, dma=False):
        amt = 16 if dma else n
        if inc is not None:
            self.cnt[inc] += amt
        if self.who != engname:
            return
        self._waits(waits)
        ins = fn(self.eng)
        if inc is not None:
            ins.then_inc(self.sems[inc], amt)

    def wait(self, engname, waits):
        if self.who != engname:
            return
        self._waits(waits)



def _host_prep(inp):
    f32 = np.float32
    H = {}
    gup = np.asarray(inp["L0_ffn_gate_up_proj"], f32)
    dn = np.asarray(inp["L0_ffn_down_proj"], f32)
    shg = np.asarray(inp["L0_ffn_shared_gate_w"], f32)
    shu = np.asarray(inp["L0_ffn_shared_up_w"], f32)
    shd = np.asarray(inp["L0_ffn_shared_down_w"], f32)
    perm = []
    for j in range(6):
        perm += list(range(j * 128, (j + 1) * 128))
        perm += list(range(768 + j * 128, 768 + (j + 1) * 128))
    gup_T = np.empty((NE, DM, 1536), f32)
    for e in range(16):
        gup_T[e] = gup[e].T[:, perm]
    sh = np.empty((DM, 1536), f32)
    sgT, suT = shg.T, shu.T
    for j in range(6):
        sh[:, (2 * j) * 128:(2 * j + 1) * 128] = sgT[:, j * 128:(j + 1) * 128]
        sh[:, (2 * j + 1) * 128:(2 * j + 2) * 128] = suT[:, j * 128:(j + 1) * 128]
    gup_T[16] = sh
    H["gup_T"] = np.ascontiguousarray(gup_T)
    dn_T = np.empty((NE, 768, DM), f32)
    for e in range(16):
        dn_T[e] = dn[e].T
    dn_T[16] = shd.T
    H["dn_T"] = np.ascontiguousarray(dn_T)

    H["router_T"] = np.ascontiguousarray(np.asarray(inp["L0_ffn_router_w"], f32).T)
    H["wq_T"] = np.ascontiguousarray(np.asarray(inp["L0_attn_Wq"], f32).T)
    H["wk_T"] = np.ascontiguousarray(np.asarray(inp["L0_attn_Wk"], f32).T)
    H["wv_T"] = np.ascontiguousarray(np.asarray(inp["L0_attn_Wv"], f32).T)
    H["wo_T"] = np.ascontiguousarray(np.asarray(inp["L0_attn_Wo"], f32).T)
    H["r_T"] = np.ascontiguousarray(np.asarray(inp["L0_attn_rand_matrix"], f32).T)

    def fm_bias(b, nch):
        return np.ascontiguousarray(np.asarray(b, f32).reshape(nch, 128).T)
    H["bq_fm"] = fm_bias(inp["L0_attn_bq"], 16)
    H["bk_fm"] = fm_bias(inp["L0_attn_bk"], 8)
    H["bq_sh"] = np.ascontiguousarray(np.roll(H["bq_fm"], -64, axis=0))
    H["bk_sh"] = np.ascontiguousarray(np.roll(H["bk_fm"], -64, axis=0))
    H["bv_fm"] = fm_bias(inp["L0_attn_bv"], 8)
    H["bo_fm"] = fm_bias(inp["L0_attn_bo"], 16)
    H["ebias_t"] = np.ascontiguousarray(
        np.broadcast_to(np.asarray(inp["L0_ffn_e_bias"], f32), (128, 16)))

    H["ident"] = np.eye(128, dtype=f32)
    H["onesf"] = np.ones((128, 1), f32)
    H["onesf_row"] = np.ones((1, 128), f32)
    H["onesbf_col"] = np.ones((128, 1), f32)
    H["onesbf_row"] = np.ones((1, 128), f32)
    oh = np.zeros((16, 2048), f32)
    for e in range(16):
        oh[e, e * 128:(e + 1) * 128] = 1
    H["onehot"] = oh
    H["desc_t"] = np.ascontiguousarray(
        np.broadcast_to((16.0 - np.arange(16, dtype=f32)), (128, 16)))
    H["ones128"] = np.ones((128, 128), f32)
    H["eps12"] = np.full((1, 1), 1e-12, f32)
    H["ebias8"] = np.ascontiguousarray(np.tile(
        np.asarray(inp["L0_ffn_e_bias"], f32)[None, :], (128, 8)))
    H["desc8"] = np.ascontiguousarray(np.tile(
        (16.0 - np.arange(16, dtype=f32))[None, :], (128, 8)))

    mc = np.empty((128, 1), f32)
    mc[0:64] = 12582912.0
    mc[64:128] = 12582912.0 + 0.25
    H["magic_col"] = mc
    sb_ = np.zeros((128, 1), f32)
    sb_[64:128] = np.pi / 2.0
    H["sinb_col"] = sb_

    invf = (1.0 / (10000.0 ** (np.arange(0, 128, 2, dtype=f32) / 128.0))).astype(f32)
    freqs = (np.arange(NTOK, dtype=f32)[:, None] * invf[None, :]).astype(f32)
    emb = np.concatenate([freqs, freqs], axis=1)
    H["cos_fm"] = np.ascontiguousarray(np.cos(emb).astype(f32).T)
    H["sin_fm"] = np.ascontiguousarray(np.sin(emb).astype(f32).T)
    H["sin_sh"] = np.ascontiguousarray(np.roll(H["sin_fm"], -64, axis=0))
    srh = H["sin_fm"].copy()
    srh[0:64] = -srh[0:64]
    H["sin_rh"] = np.ascontiguousarray(srh)
    return H



def build_nc(phases=("A", "B", "C"), dbg=False):
    nc = bass.Bass("TRN2", target_bir_lowering=False)
    D = {}
    D["x_fm"] = nc.dram_tensor("x_fm", [DM, NTOK], F32, kind="ExternalInput").ap()
    D["out_fm"] = nc.dram_tensor("out_fm", [DM, NTOK], F32, kind="ExternalOutput").ap()
    if dbg:
        D["dbg_y"] = nc.dram_tensor("dbg_y", [DM, NTOK], F32,
                                    kind="ExternalOutput").ap()
        D["dbg_comb"] = nc.dram_tensor("dbg_comb", [16, NTOK], F32,
                                       kind="ExternalOutput").ap()
        D["dbg_sc"] = nc.dram_tensor("dbg_sc", [128, 16 * 16], F32,
                                     kind="ExternalOutput").ap()
    D["x1s"] = nc.dram_tensor("x1s", [DM, NTOK], F32, kind="Internal").ap()
    D["x2s"] = nc.dram_tensor("x2s", [DM, NTOK], F32, kind="Internal").ap()
    D["svd"] = nc.dram_tensor("svd", [4, 128, 32 * 512], BF16, kind="Internal").ap()
    for name, shape in [
        ("gup_T", [NE, DM, 1536]), ("dn_T", [NE, 768, DM]),
        ("router_T", [DM, 16]), ("wq_T", [DM, DM]), ("wk_T", [DM, 1024]),
        ("wv_T", [DM, 1024]), ("wo_T", [DM, DM]), ("r_T", [128, 64]),
        ("bq_fm", [128, 16]), ("bk_fm", [128, 8]), ("bv_fm", [128, 8]),
        ("bq_sh", [128, 16]), ("bk_sh", [128, 8]),
        ("bo_fm", [128, 16]), ("ebias_t", [128, 16]), ("ident", [128, 128]),
        ("onesf", [128, 1]), ("onesf_row", [1, 128]), ("desc_t", [128, 16]),
        ("ones128", [128, 128]), ("ebias8", [128, 128]), ("desc8", [128, 128]),
        ("eps12", [1, 1]), ("magic_col", [128, 1]), ("sinb_col", [128, 1]),
        ("cos_fm", [128, NTOK]), ("sin_fm", [128, NTOK]), ("sin_rh", [128, NTOK]),
    ]:
        D[name] = nc.dram_tensor(name, shape, F32, kind="ExternalInput").ap()
    for name, shape in [
        ("onesbf_col", [128, 1]), ("onesbf_row", [1, 128]), ("onehot", [16, 2048]),
    ]:
        D[name] = nc.dram_tensor(name, shape, F32, kind="ExternalInput").ap()

    sem_es = ExitStack()
    sems = {s: sem_es.enter_context(nc.semaphore(s)) for s in SEMS}
    E = Emit(sems)

    pes = ExitStack()
    P = {}
    def psb(name, shape, dt):
        P[name] = pes.enter_context(nc.sbuf_tensor("sb_" + name, shape, dt))
    psb("ident", [128, 128], F32)
    psb("onesf", [128, 1], F32)
    psb("onesf_row", [1, 128], F32)
    psb("onesbf_col", [128, 1], BF16)
    psb("onesbf_row", [1, 128], BF16)
    psb("oh_bf", [16, 2048], BF16)
    psb("bq", [128, 16], F32)
    psb("bk", [128, 8], F32)
    psb("bq_sh", [128, 16], F32)
    psb("bk_sh", [128, 8], F32)
    psb("bv", [128, 8], F32)
    psb("bo", [128, 16], F32)
    psb("router_bf", [128, KC * 16], BF16)
    psb("rT_bf", [128, 64], BF16)
    psb("ones128", [128, 128], BF16)
    psb("ebias8", [128, 128], F32)
    psb("eps12", [1, 1], F32)
    psb("desc8", [128, 128], BF16)
    psb("ident16", [128, 128], BF16)
    psb("magic_col", [128, 1], F32)
    psb("sinb_col", [128, 1], F32)
    spans = [pes.enter_context(nc.psum_tensor(f"span{i}", [128, 1024], F32))
             for i in range(4)]
    banks = [spans[i // 2][:, (i % 2) * 512:(i % 2) * 512 + 512]
             for i in range(8)]

    def run_block(blk, schedule):
        engmap = [("sp", blk.sync), ("pool", blk.gpsimd), ("pe", blk.tensor),
                  ("act", blk.scalar), ("dve", blk.vector)]
        for who, starter in engmap:
            def runner(eng, who=who):
                E.start(who, eng)
                schedule()
                E.finish()
            starter(runner)

    def sched_prologue():
        pairs = [("ident", "ident"), ("onesf", "onesf"), ("onesf_row", "onesf_row"),
                 ("bq", "bq_fm"), ("bk", "bk_fm"), ("bv", "bv_fm"), ("bo", "bo_fm"),
                 ("bq_sh", "bq_sh"), ("bk_sh", "bk_sh"),
                 ("ebias8", "ebias8"), ("eps12", "eps12"),
                 ("magic_col", "magic_col"), ("sinb_col", "sinb_col")]
        for dst, src in pairs:
            E.op("sp", lambda sy, d=dst, s=src: sy.dma_start(out=P[d][:], in_=D[s]),
                 inc="s_cast", dma=True)
        for dst, src in [("onesbf_col", "onesbf_col"), ("onesbf_row", "onesbf_row"),
                         ("oh_bf", "onehot"), ("ones128", "ones128"),
                         ("desc8", "desc8"), ("ident16", "ident")]:
            E.op("pool", lambda g, d=dst, s=src: g.dma_start(out=P[d][:], in_=D[s]),
                 inc="s_castp", dma=True)
        E.op("pool", lambda g: g.dma_start(
            out=P["router_bf"][:].rearrange("p (k n) -> p k n", k=KC),
            in_=D["router_T"].rearrange("(k p) n -> p k n", p=128)),
            inc="s_castp", dma=True)
        E.op("pool", lambda g: g.dma_start(out=P["rT_bf"][:], in_=D["r_T"]),
             inc="s_castp", dma=True)
        E.snap("pro_sp", "s_cast")
        E.snap("pro_pool", "s_castp")

    with nc.Block("prologue") as blk:
        run_block(blk, sched_prologue)

    def em_ln(tag, xs_view, y_view, nts, T, waits0):
        rows, scr2 = T["rows"], T["scr2"]
        first = {k: list(v) for k, v in waits0.items()}
        def w0(eng):
            return first.pop(eng, [])
        sl = lambda i: rows[0:1, i * 512:(i + 1) * 512]
        for ts in range(nts):
            pe_extra = [f"{tag}y{ts-1}"] if ts > 0 else []
            for c in range(KC):
                E.op("act", lambda s, c=c, ts=ts: s.activation(
                    scr2[:, (c % 2) * 512:(c % 2) * 512 + 512], xs_view(c, ts),
                    AF.Square),
                    waits=w0("act") + ([f"{tag}sqm{ts}_{c-2}"] if c >= 2 else []),
                    inc="s_act")
                E.snap(f"{tag}sq{ts}_{c}", "s_act")
                E.op("pe", lambda t, c=c, ts=ts: t.matmul(
                    banks[6][0:1, :], P["onesf"][:], xs_view(c, ts),
                    start=(c == 0), stop=(c == KC - 1)),
                    waits=(w0("pe") + pe_extra) if c == 0 else [])
                E.op("pe", lambda t, c=c: t.matmul(
                    banks[7][0:1, :], P["onesf"][:],
                    scr2[:, (c % 2) * 512:(c % 2) * 512 + 512],
                    start=(c == 0), stop=(c == KC - 1)),
                    waits=[f"{tag}sq{ts}_{c}"], inc="s_pe")
                E.snap(f"{tag}sqm{ts}_{c}", "s_pe")
            E.snap(f"{tag}stat{ts}", "s_pe")
            E.op("dve", lambda v: v.tensor_scalar_mul(sl(0), banks[6][0:1, :], 1.0 / DM),
                 waits=w0("dve") + [f"{tag}stat{ts}"], inc="s_dve")
            E.op("dve", lambda v: v.tensor_scalar_mul(sl(1), banks[7][0:1, :], 1.0 / DM))
            E.op("dve", lambda v: v.tensor_mul(sl(2), sl(0), sl(0)))
            E.op("dve", lambda v: v.tensor_sub(sl(1), sl(1), sl(2)))
            E.op("dve", lambda v: v.tensor_single_scalar(sl(1), sl(1), LN_EPS, ALU.add),
                 inc="s_dve")
            E.snap(f"{tag}var{ts}", "s_dve")
            E.op("act", lambda s: s.activation(sl(2), sl(1), AF.Ln),
                 waits=[f"{tag}var{ts}"])
            E.op("act", lambda s: s.activation(sl(1), sl(2), AF.Exp, scale=-0.5),
                 inc="s_act")
            E.snap(f"{tag}rstd{ts}", "s_act")
            E.op("dve", lambda v: v.scalar_tensor_tensor(
                sl(0), sl(0), -1.0, sl(1), ALU.mult, ALU.mult),
                waits=[f"{tag}rstd{ts}"], inc="s_dve")
            E.snap(f"{tag}brow{ts}", "s_dve")
            E.op("pe", lambda t: t.matmul(banks[6][:, :], P["onesf_row"][:], sl(1),
                                          start=True, stop=True),
                 waits=[f"{tag}brow{ts}"])
            E.op("pe", lambda t: t.matmul(banks[7][:, :], P["onesf_row"][:], sl(0),
                                          start=True, stop=True), inc="s_pe")
            E.snap(f"{tag}bc{ts}", "s_pe")
            for c in range(KC):
                E.op("dve", lambda v, c=c, ts=ts: v.tensor_mul(
                    T["t1"][:], xs_view(c, ts), banks[6][:, :]),
                    waits=([f"{tag}bc{ts}"] if c == 0 else []))
                E.op("dve", lambda v, c=c, ts=ts: v.tensor_add(
                    y_view(c, ts), T["t1"][:], banks[7][:, :]), inc="s_dve")
            E.snap(f"{tag}y{ts}", "s_dve")
        E.snap(f"{tag}ydone", "s_dve")

    def alloc_ffn(es, pref):
        T = {}
        def sb(name, shape, dt):
            T[name] = es.enter_context(nc.sbuf_tensor(pref + name, shape, dt))
        sb("xs", [128, KC * MACRO], F32)
        sb("y", [128, KC * MACRO], BF16)
        sb("wA", [128, KC * 768], BF16)
        sb("wB", [128, KC * 768], BF16)
        sb("wD", [128, 6 * DM], BF16)
        sb("h", [128, 6 * MACRO], BF16)
        sb("c_sb", [128, 2 * MACRO], BF16)
        sb("scr2", [128, 2 * 512], F32)
        sb("t1", [128, 512], F32)
        sb("rows", [1, 3 * 512], F32)
        sb("scores", [128, 8 * 16], F32)
        sb("tk", [128, 3 * 128 + 32], F32)
        sb("combine_bf", [16, MACRO], BF16)
        return T

    def sched_ffn(tag, T, src, dst, start_waits):
        src3 = src.rearrange("(c p) t -> p c t", p=128)
        dst3 = dst.rearrange("(c p) t -> p c t", p=128)
        for eng in ("sp", "pool", "pe", "act", "dve"):
            E.wait(eng, list(start_waits) +
                   [("s_cast", E.snaps["pro_sp"][1]),
                    ("s_castp", E.snaps["pro_pool"][1])])
        for m in range(NMACRO):
            mt = f"{tag}{m}"
            pv = f"{tag}{m-1}"
            E.op("sp", lambda sy, m=m: sy.dma_start(
                out=T["xs"][:].rearrange("p (c t) -> p c t", c=KC),
                in_=src3[:, :, m * MACRO:(m + 1) * MACRO]),
                waits=([f"{pv}st"] if m > 0 else []),
                inc="s_xin", dma=True)
            E.snap(f"{mt}xl", "s_xin")

            xs_view = lambda c, ts: T["xs"][:, c * MACRO + ts * 512:
                                            c * MACRO + ts * 512 + 512]
            y_view = lambda c, ts: T["y"][:, c * MACRO + ts * 512:
                                          c * MACRO + ts * 512 + 512]
            w0 = {"pe": [f"{mt}xl"] + ([f"{pv}dnevL"] if m > 0 else []),
                  "act": [f"{mt}xl"] + ([f"{pv}sgfree"] if m > 0 else []),
                  "dve": [f"{mt}xl"] +
                         ([f"{pv}e{NE-1}guwBdone"] if m > 0 else [])}
            em_ln(mt, xs_view, y_view, 2, T, w0)

            for tc in range(8):
                bk = banks[6 + (tc % 2)]
                rw = [f"{mt}ydone"] + ([f"{mt}sig{tc-2}"] if tc >= 2 else [])
                for k in range(KC):
                    E.op("pe", lambda t, tc=tc, k=k, bk=bk: t.matmul(
                        bk[:, 0:16],
                        T["y"][:, k * MACRO + tc * 128: k * MACRO + tc * 128 + 128],
                        P["router_bf"][:, k * 16:(k + 1) * 16],
                        start=(k == 0), stop=(k == KC - 1)),
                        waits=(rw if k == 0 else []),
                        inc=("s_pe" if k == KC - 1 else None))
                E.snap(f"{mt}rt{tc}", "s_pe")
                E.op("act", lambda s, tc=tc, bk=bk: s.activation(
                    T["scores"][:, tc * 16:(tc + 1) * 16], bk[:, 0:16], AF.Sigmoid),
                    waits=[f"{mt}rt{tc}", f"{mt}ydone"], inc="s_act")
                E.snap(f"{mt}sig{tc}", "s_act")
            tk = T["tk"]
            seg = lambda ap: ap.rearrange("p (s e) -> p s e", e=16)
            bsc = tk[:, 0:128]; msk = tk[:, 128:256]; val = tk[:, 256:384]
            cw = tk[:, 0:128]; comb8 = tk[:, 256:384]
            m1 = tk[:, 384:392]; ws = tk[:, 392:400]; iv = tk[:, 400:408]
    
            def dsync(fn, waits=()):
                E.op("dve", fn,
                     waits=list(waits) + [("s_dve", E.cnt["s_dve"])],
                     inc="s_dve")
            bc = lambda ap: ap.unsqueeze(2).broadcast_to([128, 8, 16])
            tw = [f"{mt}sig7"]
            if m > 0:
                tw.append(f"{pv}ctpL")
            dsync(lambda v: v.tensor_add(bsc, T["scores"][:], P["ebias8"][:]),
                  waits=tw)
            dsync(lambda v: v.memset(msk, 0.0))
            for it in range(4):
                dsync(lambda v: v.reduce_max(m1, seg(bsc), axis=AX.X))
                dsync(lambda v: v.tensor_tensor(seg(val), seg(bsc), bc(m1),
                                                ALU.is_ge))
                dsync(lambda v: v.tensor_mul(val, val, P["desc8"][:]))
                dsync(lambda v: v.reduce_max(m1, seg(val), axis=AX.X))
                dsync(lambda v: v.tensor_tensor(seg(val), seg(val), bc(m1),
                                                ALU.is_equal))
                dsync(lambda v: v.tensor_add(msk, msk, val))
                dsync(lambda v: v.scalar_tensor_tensor(
                    bsc, val, -1e30, bsc, ALU.mult, ALU.add))
            dsync(lambda v: v.tensor_mul(cw, msk, T["scores"][:]))
            dsync(lambda v: v.reduce_sum(ws, seg(cw), axis=AX.X))
            dsync(lambda v: v.tensor_single_scalar(ws, ws, 1e-20, ALU.add))
            dsync(lambda v: v.reciprocal(iv, ws))
            dsync(lambda v: v.scalar_tensor_tensor(
                seg(comb8), seg(cw), 0.5, bc(iv), ALU.mult, ALU.mult))
            E.snap(f"{mt}comb", "s_dve")
            for tc in range(8):
                ctw = [f"{mt}comb"]
                if tc >= 1:
                    ctw.append(f"{mt}cbf{tc-1}")
                elif m > 0:
                    ctw += [f"{pv}silu{NPM-1}", f"{pv}hp{NPM-1}"]
                E.op("pe", lambda t, tc=tc: t.transpose(
                    banks[tc // 2][0:16, (tc % 2) * 128:(tc % 2) * 128 + 128],
                    comb8[:, tc * 16:(tc + 1) * 16], P["ident"][:]),
                    waits=ctw, inc="s_pe")
                E.snap(f"{mt}ctp{tc}", "s_pe")
                E.op("act", lambda s, tc=tc: s.activation(
                    T["combine_bf"][0:16, tc * 128:(tc + 1) * 128],
                    banks[tc // 2][0:16, (tc % 2) * 128:(tc % 2) * 128 + 128],
                    AF.Copy),
                    waits=[f"{mt}ctp{tc}"], inc="s_act")
                E.snap(f"{mt}cbf{tc}", "s_act")
            E.snap(f"{mt}ctpL", "s_pe")
            E.snap(f"{mt}cbfL", "s_act")
            if dbg and tag == "A":
                E.op("pool", lambda g, m=m: g.dma_start(
                    out=D["dbg_y"].rearrange("(c p) t -> p c t", p=128)
                    [:, :, m * MACRO:(m + 1) * MACRO],
                    in_=T["y"][:].rearrange("p (c t) -> p c t", c=KC)),
                    waits=[f"{mt}ydone"], inc="s_misc", dma=True)
                E.op("pool", lambda g, m=m: g.dma_start(
                    out=D["dbg_comb"][0:16, m * MACRO:(m + 1) * MACRO],
                    in_=T["combine_bf"][0:16, :]),
                    waits=[f"{mt}cbfL"], inc="s_misc", dma=True)
                E.op("pool", lambda g, m=m: g.dma_start(
                    out=D["dbg_sc"][:, m * 128:(m + 1) * 128],
                    in_=T["scores"][:]),
                    waits=[f"{mt}cbfL"], inc="s_misc", dma=True)
                E.snap(f"{mt}dbg", "s_misc")
                E.wait("sp", [f"{mt}dbg"])

            pair_ctr = 0
            for e in range(NE):
                et = f"{mt}e{e}"
                for role, semn, cols in (("wA", "s_wA", (0, 768)),
                                         ("wB", "s_wB", (768, 1536))):
                    kp = (f"{mt}e{e-1}gu{role}done" if e > 0 else
                          (f"{pv}e{NE-1}gu{role}done" if m > 0 else None))
                    E.op("pool", lambda g, e=e, cols=cols, role=role: g.dma_start(
                        out=T[role][:].rearrange("p (k n) -> p k n", k=KC),
                        in_=D["gup_T"][e, :, cols[0]:cols[1]].rearrange(
                            "(k p) n -> p k n", p=128)),
                        waits=([kp] if kp else []), inc=semn, dma=True)
                    E.snap(f"{et}{role}ld", semn)
                kp = (f"{mt}e{e-1}dnL" if e > 0 else
                      (f"{pv}e{NE-1}dnL" if m > 0 else None))
                E.op("pool", lambda g, e=e: g.dma_start(
                    out=T["wD"][:].rearrange("p (k n) -> p k n", k=6),
                    in_=D["dn_T"][e].rearrange("(k p) n -> p k n", p=128)),
                    waits=([kp] if kp else []), inc="s_wD", dma=True)
                E.snap(f"{et}wDld", "s_wD")

                if e < 16:
                    for ts in range(2):
                        if ts == 0:
                            cwaits = [f"{mt}cbfL"]
                            if e > 0:
                                cwaits.append(f"{mt}e{e-1}dnevL")
                            elif m > 0:
                                cwaits.append(f"{pv}dnevL")
                        else:
                            cwaits = [f"{et}cev0"]
                        E.op("pe", lambda t, e=e, ts=ts: t.matmul(
                            banks[6][:, :], P["oh_bf"][0:16, e * 128:(e + 1) * 128],
                            T["combine_bf"][0:16, ts * 512:(ts + 1) * 512],
                            start=True, stop=True),
                            waits=cwaits, inc="s_pe")
                        E.snap(f"{et}cmm{ts}", "s_pe")
                        hw = [f"{et}cmm{ts}"]
                        if e >= 2:
                            hw.append(f"{mt}e{e-2}hL")
                        elif m > 0:
                            hw.append(f"{pv}e{14 + e}hL")
                        E.op("act", lambda s, e=e, ts=ts: s.activation(
                            T["c_sb"][:, (e % 2) * MACRO + ts * 512:
                                      (e % 2) * MACRO + ts * 512 + 512],
                            banks[6][:, :], AF.Copy),
                            waits=hw, inc="s_act")
                        E.snap(f"{et}cev{ts}", "s_act")

                for gi, (hi, ts) in enumerate([(0, 0), (0, 1), (1, 0), (1, 1)]):
                    role = "wA" if hi == 0 else "wB"
                    gwaits = [f"{et}{role}ld", f"{mt}ydone"]
                    if pair_ctr >= 3:
                        gwaits += [f"{mt}silu{pair_ctr-1}", f"{mt}hp{pair_ctr-1}"]
                    elif m > 0:
                        gwaits += [f"{pv}silu{NPM-1}", f"{pv}hp{NPM-1}"]
                    if e == 0 and gi == 0:
                        gwaits.append(f"{mt}cbfL")
                    for ofp in range(6):
                        for k in range(KC):
                            E.op("pe", lambda t, role=role, ofp=ofp, k=k, ts=ts:
                                 t.matmul(
                                     banks[ofp][:, :],
                                     T[role][:, k * 768 + ofp * 128:
                                             k * 768 + (ofp + 1) * 128],
                                     T["y"][:, k * MACRO + ts * 512:
                                            k * MACRO + ts * 512 + 512],
                                     start=(k == 0), stop=(k == KC - 1)),
                                 waits=(gwaits if (ofp == 0 and k == 0) else []),
                                 inc=("s_pe" if k == KC - 1 else None))
                        E.snap(f"{et}gu{gi}_{ofp}", "s_pe")
                    if hi == 0 and ts == 1:
                        E.snap(f"{et}guwAdone", "s_pe")
                    if hi == 1 and ts == 1:
                        E.snap(f"{et}guwBdone", "s_pe")
                    for j in range(3):
                        pj = pair_ctr
                        pair_ctr += 1
                        swaits = [f"{et}gu{gi}_{2*j}"]
                        if pj >= 2:
                            swaits.append(f"{mt}hp{pj-2}")
                        E.op("act", lambda s, pj=pj, j=j: s.activation(
                            T["scr2"][:, (pj % 2) * 512:(pj % 2) * 512 + 512],
                            banks[2 * j][:, :], AF.Silu),
                            waits=swaits, inc="s_act")
                        E.snap(f"{mt}silu{pj}", "s_act")
                        hwaits = [f"{mt}silu{pj}", f"{et}gu{gi}_{2*j+1}"]
                        if e < 16:
                            hwaits.append(f"{et}cev1")
                            E.op("dve", lambda v, pj=pj, e=e, ts=ts: v.tensor_mul(
                                T["t1"][:],
                                T["scr2"][:, (pj % 2) * 512:(pj % 2) * 512 + 512],
                                T["c_sb"][:, (e % 2) * MACRO + ts * 512:
                                          (e % 2) * MACRO + ts * 512 + 512]),
                                waits=hwaits)
                        else:
                            E.op("dve", lambda v, pj=pj: v.tensor_scalar_mul(
                                T["t1"][:],
                                T["scr2"][:, (pj % 2) * 512:(pj % 2) * 512 + 512],
                                0.5),
                                waits=hwaits)
                        hcol = (hi * 3 + j) * MACRO + ts * 512
                        E.op("dve", lambda v, hcol=hcol, j=j: v.tensor_mul(
                            T["h"][:, hcol:hcol + 512], T["t1"][:],
                            banks[2 * j + 1][:, :]),
                            inc="s_dve")
                        E.snap(f"{mt}hp{pj}", "s_dve")
                E.snap(f"{mt}e{e}hL", "s_dve")

                for ts in range(2):
                    for of in range(16):
                        didx = e * 32 + ts * 16 + of
                        bk = banks[6 + (didx % 2)]
                        dwaits = [f"{et}wDld", f"{mt}e{e}hL"]
                        if ts == 0 and of == 0 and e < 16:
                            dwaits.append(f"{et}cev1")
                        if didx >= 2:
                            dwaits.append(f"{mt}dnev{didx-2}")
                        elif m > 0:
                            dwaits.append(f"{pv}dnevL")
                        for j in range(6):
                            E.op("pe", lambda t, bk=bk, j=j, of=of, ts=ts: t.matmul(
                                bk[:, :],
                                T["wD"][:, j * DM + of * 128: j * DM + (of + 1) * 128],
                                T["h"][:, j * MACRO + ts * 512:
                                       j * MACRO + ts * 512 + 512],
                                start=(j == 0), stop=(j == 5)),
                                waits=(dwaits if j == 0 else []),
                                inc=("s_pe" if j == 5 else None))
                        E.snap(f"{mt}dnb{didx}", "s_pe")
                        E.op("dve", lambda v, bk=bk, of=of, ts=ts: v.tensor_add(
                            T["xs"][:, of * MACRO + ts * 512:
                                    of * MACRO + ts * 512 + 512],
                            T["xs"][:, of * MACRO + ts * 512:
                                    of * MACRO + ts * 512 + 512],
                            bk[:, :]),
                            waits=[f"{mt}dnb{didx}"], inc="s_dve")
                        E.snap(f"{mt}dnev{didx}", "s_dve")
                E.snap(f"{et}dnL", "s_pe")
                E.snap(f"{mt}e{e}dnevL", "s_dve")
            E.snap(f"{mt}dnevL", "s_dve")
            E.snap(f"{mt}sgfree", "s_dve")
            E.op("sp", lambda sy, m=m: sy.dma_start(
                out=dst3[:, :, m * MACRO:(m + 1) * MACRO],
                in_=T["xs"][:].rearrange("p (c t) -> p c t", c=KC)),
                waits=[f"{mt}dnevL"], inc="s_st", dma=True)
            E.snap(f"{mt}st", "s_st")
        E.snap(f"{tag}end", "s_st")

    def alloc_attn_B(es):
        T = {}
        def sb(name, shape, dt):
            T[name] = es.enter_context(nc.sbuf_tensor("at_" + name, shape, dt))
        sb("xstrip", [128, KC * 512], F32)
        sb("blob0", [128, 32 * 512], BF16)
        sb("blob1", [128, 32 * 512], BF16)
        sb("wring", [128, 5 * KC * 128], BF16)
        sb("wob", [128, 2 * KC * 128], BF16)
        sb("x1o", [128, 2 * 512], F32)
        sb("x2o", [128, 2 * 512], F32)
        sb("cosb", [128, NTOK], BF16)
        sb("sinrhb", [128, NTOK], BF16)
        sb("scrB", [128, 2 * 512], F32)
        sb("d1", [128, 512], F32)
        sb("t2", [128, 512], F32)
        sb("sq32", [128, 2 * 512], F32)
        sb("p_sb", [128, 2 * 512], F32)
        sb("rsc", [128, 512], F32)
        sb("kkb", [128, 512], F32)
        sb("frac", [128, 2 * 512], F32)
        sb("krv", [128, 2 * 512], BF16)
        sb("qp", [128, 2 * 512], BF16)
        sb("prodb", [128, 2 * 1024], BF16)
        sb("tsb", [128, 2 * 1024], BF16)
        sb("scr2", [128, 2 * 512], F32)
        sb("t1", [128, 512], F32)
        sb("rows", [1, 3 * 512], F32)
        sb("brow", [1, 4 * 512], F32)
        sb("lrow", [1, 512], F32)
        return T

    def sched_attn_B(T, dst):
        sw = [("s_st", E.snaps["Aend"][1])] if "Aend" in E.snaps else []
        for eng in ("sp", "pool", "pe", "act", "dve"):
            E.wait(eng, sw + [("s_cast", E.snaps["pro_sp"][1]),
                              ("s_castp", E.snaps["pro_pool"][1])])
        x13 = D["x1s"].rearrange("(c p) t -> p c t", p=128)
        dst3 = dst.rearrange("(c p) t -> p c t", p=128)
        zgbits = T["xstrip"][:].bitcast(BF16)
        blob = [T["blob0"][:], T["blob1"][:]]

        E.op("pool", lambda g: g.dma_start(out=T["cosb"][:], in_=D["cos_fm"]),
             inc="s_misc", dma=True)
        E.snap("Btab1", "s_misc")
        E.op("pool", lambda g: g.dma_start(out=T["sinrhb"][:], in_=D["sin_rh"]),
             inc="s_cast", dma=True)
        E.snap("Btab2", "s_cast")

        def rope_da(pfx, fi, pa, bcol, bshcol, s, prev2, pjsnap):
            par = fi % 2
            sB = T["scrB"][:, par * 512:par * 512 + 512]
            cs = T["cosb"][:, s * 512:(s + 1) * 512]
            sn = T["sinrhb"][:, s * 512:(s + 1) * 512]
            E.op("dve", lambda v: v.scalar_tensor_tensor(
                T["d1"][:], pa, bcol, cs, ALU.add, ALU.mult),
                waits=[pjsnap, "Btab1"], inc="s_dve")
            E.snap(f"{pfx}d1", "s_dve")
            raw = [pjsnap, f"{pfx}d1"]
            if prev2 is not None:
                raw.append(f"{prev2}t2")
            E.op("act", lambda a: a.activation(
                sB[0:64, :], pa[64:128, :], AF.Identity, bias=bshcol[0:64, 0:1]),
                waits=raw)
            E.op("act", lambda a: a.activation(
                sB[64:128, :], pa[0:64, :], AF.Identity,
                bias=bshcol[64:128, 0:1]), inc="s_act")
            E.snap(f"{pfx}ra", "s_act")
            E.op("dve", lambda v: v.scalar_tensor_tensor(
                T["t2"][:], sB, 1.0, sn, ALU.mult, ALU.mult),
                waits=[f"{pfx}ra", "Btab2"], inc="s_dve")
            E.snap(f"{pfx}t2", "s_dve")
            kw = []
            if prev2 is not None:
                kw += [f"{prev2}pm", f"{prev2}sq"]
            E.op("dve", lambda v, par=par: v.tensor_tensor(
                T["krv"][:, par * 512:par * 512 + 512], T["d1"][:], T["t2"][:],
                ALU.add), waits=kw, inc="s_dve")
            E.snap(f"{pfx}kv", "s_dve")

        def sq_act(pfx, fi, prev2):
            par = fi % 2
            w = [f"{pfx}kv"]
            if prev2 is not None:
                w.append(f"{prev2}nm")
            E.op("act", lambda a: a.activation(
                T["sq32"][:, par * 512:par * 512 + 512],
                T["krv"][:, par * 512:par * 512 + 512], AF.Square),
                waits=w, inc="s_act")
            E.snap(f"{pfx}sq", "s_act")

        def nmm_pe2(pfx, fi, nb, nbw):
            par = fi % 2
            E.op("pe", lambda t, par=par: t.matmul(
                nb[0:1, :], P["onesf"][:],
                T["sq32"][:, par * 512:par * 512 + 512], start=True, stop=True),
                waits=[f"{pfx}sq"] + nbw, inc="s_pe")
            E.snap(f"{pfx}nm", "s_pe")

        def lnexp_act(pfx, bslot, browf):
            E.op("act", lambda a: a.activation(
                T["lrow"][0:1, :], nb_of[pfx][0:1, :], AF.Ln,
                bias=P["eps12"][0:1, 0:1]),
                waits=[f"{pfx}nm"] + browf)
            E.op("act", lambda a, bslot=bslot: a.activation(
                T["brow"][0:1, bslot * 512:(bslot + 1) * 512], T["lrow"][0:1, :],
                AF.Exp, scale=-0.5), inc="s_act")
            E.snap(f"{pfx}ln", "s_act")

        def pmm_pe(pfx, fi, nb):
            par = fi % 2
            E.op("pe", lambda t, par=par: t.matmul(
                nb[0:64, :], P["rT_bf"][:],
                T["krv"][:, par * 512:par * 512 + 512], start=True, stop=True),
                waits=[f"{pfx}kv", f"{pfx}ln"], inc="s_pe")
            E.snap(f"{pfx}pm", "s_pe")

        def pev_act(pfx, fi, nb, prev2):
            par = fi % 2
            pv = T["p_sb"][:, par * 512:par * 512 + 512]
            w = [f"{pfx}pm"]
            if prev2 is not None:
                w.append(f"{prev2}rs")
            E.op("act", lambda a: a.activation(pv[0:64, :], nb[0:64, :], AF.Copy),
                 waits=w)
            E.op("act", lambda a: a.activation(pv[64:128, :], nb[0:64, :],
                                               AF.Copy), inc="s_act")
            E.snap(f"{pfx}pv", "s_act")

        def invb_pe(pfx, bslot, nb):
            E.op("pe", lambda t, bslot=bslot: t.matmul(
                nb, P["onesf_row"][:],
                T["brow"][0:1, bslot * 512:(bslot + 1) * 512],
                start=True, stop=True),
                waits=[f"{pfx}ln", f"{pfx}pv"], inc="s_pe")
            E.snap(f"{pfx}iv", "s_pe")

        def range_dve(pfx, fi, nb, prev2):
            par = fi % 2
            pv = T["p_sb"][:, par * 512:par * 512 + 512]
            E.op("dve", lambda v: v.scalar_tensor_tensor(
                T["rsc"][:], pv, INV2PI, nb, ALU.mult, ALU.mult),
                waits=[f"{pfx}iv", f"{pfx}pv"], inc="s_dve")
            E.snap(f"{pfx}rs", "s_dve")
            E.op("dve", lambda v: v.tensor_scalar(
                T["kkb"][:], T["rsc"][:], P["magic_col"][:, 0:1], MAGIC,
                ALU.add, ALU.subtract))
            fw = []
            if prev2 is not None:
                fw.append(f"{prev2}sn")
            E.op("dve", lambda v: v.scalar_tensor_tensor(
                T["frac"][:, par * 512:par * 512 + 512], T["kkb"][:], -1.0,
                T["rsc"][:], ALU.mult, ALU.add), waits=fw, inc="s_dve")
            E.snap(f"{pfx}fr", "s_dve")

        def sin_act(pfx, fi, feat_dst, featw):
            par = fi % 2
            E.op("act", lambda a: a.activation(
                feat_dst, T["frac"][:, par * 512:par * 512 + 512], AF.Sin,
                bias=P["sinb_col"][:, 0:1], scale=TWO_PI),
                waits=[f"{pfx}fr"] + featw, inc="s_act")
            E.snap(f"{pfx}sn", "s_act")

        nb_of = {}

        for s in range(4):
            xlw = [f"BN{s-1}ydone"] if s > 0 else []
            E.op("sp", lambda sy, s=s: sy.dma_start(
                out=T["xstrip"][:].rearrange("p (c t) -> p c t", c=KC),
                in_=x13[:, :, s * 512:(s + 1) * 512]),
                waits=xlw, inc="s_xin", dma=True)
            E.snap(f"Cxl{s}", "s_xin")
            svw = [f"SV{s-2}"] if s >= 2 else []
            w0 = {"pe": [f"Cxl{s}"] + ([f"BN{s-1}ydone"] if s > 0 else []),
                  "act": [f"Cxl{s}"] + svw,
                  "dve": [f"Cxl{s}"] + svw}
            xs_view = lambda c, ts: T["xstrip"][:, c * 512:(c + 1) * 512]
            y_view = lambda c, ts, s=s: blob[s % 2][:, c * 512:(c + 1) * 512]
            em_ln(f"BN{s}", xs_view, y_view, 1, T, w0)

            for slot in range(11):
                if slot < 8:
                    ki = slot
                    ik = s * 8 + ki
                    for kind in ("k", "v"):
                        u = 2 * ki + (0 if kind == "k" else 1)
                        wsl = u % 4
                        wname = "wk_T" if kind == "k" else "wv_T"
                        lw = []
                        if u >= 4:
                            lw.append(f"Cpj{s}_{u-4}")
                        elif s > 0:
                            lw.append(f"Cpj{s-1}_{u+12}")
                        E.op("pool", lambda g, wsl=wsl, wname=wname, ki=ki:
                             g.dma_start(
                                 out=T["wring"][:, wsl * 2048:(wsl + 1) * 2048]
                                 .rearrange("p (k n) -> p k n", k=KC),
                                 in_=D[wname][:, ki * 128:(ki + 1) * 128]
                                 .rearrange("(k p) n -> p k n", p=128)),
                             waits=lw, inc=WRING[wsl], dma=True)
                        E.snap(f"Cw{s}_{u}", WRING[wsl])
                        pa = banks[0] if kind == "k" else banks[1]
                        pw = [f"Cw{s}_{u}"]
                        if u < 2:
                            pw.append(f"BN{s}ydone")
                        if kind == "k":
                            if ik > 0:
                                pw += [f"C{ik-1}ra", f"C{ik-1}d1"]
                        else:
                            if ik > 0:
                                pw.append(f"Cev{ik-1}")
                        for k in range(KC):
                            E.op("pe", lambda t, wsl=wsl, k=k, pa=pa, s=s:
                                 t.matmul(
                                     pa,
                                     T["wring"][:, wsl * 2048 + k * 128:
                                                wsl * 2048 + (k + 1) * 128],
                                     blob[s % 2][:, k * 512:(k + 1) * 512],
                                     start=(k == 0), stop=(k == KC - 1)),
                                 waits=(pw if k == 0 else []),
                                 inc=("s_pe" if k == KC - 1 else None))
                        E.snap(f"Cpj{s}_{u}", "s_pe")
                        if kind == "k":
                            pfx = f"C{ik}"
                            nb_of[pfx] = banks[2 + ik % 4]
                            prev2 = f"C{ik-2}" if ik >= 2 else None
                            rope_da(pfx, ik, pa, P["bk"][:, ki:ki + 1],
                                    P["bk_sh"][:, ki:ki + 1], s, prev2,
                                    f"Cpj{s}_{u}")
                            sq_act(pfx, ik, prev2)
                        else:
                            vw = [f"Cpj{s}_{u}"]
                            if ik >= 2:
                                vw.append(f"Cev{ik-2}")
                            E.op("act", lambda a, s=s, ki=ki, pa=pa:
                                 a.activation(
                                     blob[s % 2][:, (24 + ki) * 512:
                                                 (25 + ki) * 512],
                                     pa, AF.Identity,
                                     bias=P["bv"][:, ki:ki + 1]),
                                 waits=vw, inc="s_act")
                            E.snap(f"Cev{ik}", "s_act")
                if 1 <= slot:
                    j = s * 8 + slot - 1
                    if slot - 1 < 8:
                        pfx = f"C{j}"
                        nbw = [f"C{j-4}rs"] if j >= 4 else []
                        nmm_pe2(pfx, j, nb_of[pfx], nbw)
                        lnexp_act(pfx, j % 4,
                                  [f"C{j-4}iv"] if j >= 4 else [])
                if 2 <= slot:
                    j = s * 8 + slot - 2
                    if slot - 2 < 8:
                        pfx = f"C{j}"
                        pmm_pe(pfx, j, nb_of[pfx])
                        pev_act(pfx, j, nb_of[pfx],
                                f"C{j-2}" if j >= 2 else None)
                if 3 <= slot:
                    j = s * 8 + slot - 3
                    if slot - 3 < 8:
                        pfx = f"C{j}"
                        invb_pe(pfx, j % 4, nb_of[pfx])
                        range_dve(pfx, j, nb_of[pfx],
                                  f"C{j-2}" if j >= 2 else None)
                        sin_act(pfx, j,
                                blob[s % 2][:, (16 + j % 8) * 512:
                                            (17 + j % 8) * 512], [])
            E.op("sp", lambda sy, s=s: sy.dma_start(
                out=D["svd"][s], in_=blob[s % 2]),
                waits=[f"C{s*8+7}sn", f"Cev{s*8+7}", f"BN{s}ydone"],
                inc=f"s_sv{s%2}", dma=True)
            E.snap(f"SV{s}", f"s_sv{s%2}")

        def reload(t):
            w = ([f"QpL{t-2}", f"QdL{t-2}"] if t >= 2 else
                 [f"Cpj{t+2}_{15}", f"SV{t+2}"])
            E.op("pool", lambda g, t=t: g.dma_start(
                out=blob[t % 2], in_=D["svd"][t % 4]),
                waits=w, inc=f"s_r{t%2}", dma=True)
            E.snap(f"R{t}", f"s_r{t%2}")

        def qw_load(h, w):
            wsl = h % 5
            E.op("pool", lambda g, h=h, wsl=wsl: g.dma_start(
                out=T["wring"][:, wsl * 2048:(wsl + 1) * 2048]
                .rearrange("p (k n) -> p k n", k=KC),
                in_=D["wq_T"][:, h * 128:(h + 1) * 128]
                .rearrange("(k p) n -> p k n", p=128)),
                waits=w, inc=WRING[wsl], dma=True)
            E.snap(f"QW{h}", WRING[wsl])

        reload(0)
        reload(1)
        for h in range(5):
            qw_load(h, [f"Cpj3_{12+h}"] if h < 4 else [])

        def q_proj(t, hh, bankw):
            g, s = t // 4, t % 4
            h = g * 4 + hh
            qu = t * 4 + hh
            pa = banks[hh % 2]
            pw = [f"QW{h}"] + bankw
            if hh == 0:
                pw.append(f"R{t}")
            for k in range(KC):
                E.op("pe", lambda tt, h=h, k=k, pa=pa, s=s: tt.matmul(
                    pa,
                    T["wring"][:, (h % 5) * 2048 + k * 128:
                               (h % 5) * 2048 + (k + 1) * 128],
                    blob[s % 2][:, k * 512:(k + 1) * 512],
                    start=(k == 0), stop=(k == KC - 1)),
                    waits=(pw if k == 0 else []),
                    inc=("s_pe" if k == KC - 1 else None))
            E.snap(f"Q{qu}pj", "s_pe")
            if hh == 3:
                E.snap(f"QpL{t}", "s_pe")

        def q_rope(t, hh):
            g, s = t // 4, t % 4
            h = g * 4 + hh
            qu = t * 4 + hh
            pfx = f"Q{qu}"
            pa = banks[hh % 2]
            nb_of[pfx] = banks[6 + hh % 2]
            prev2 = f"Q{qu-2}" if qu >= 2 else f"C{30+qu}"
            rope_da(pfx, qu, pa, P["bq"][:, h:h + 1], P["bq_sh"][:, h:h + 1],
                    s, prev2, f"Q{qu}pj")
            sq_act(pfx, qu, prev2)

        def q_norm_a(t, hh):
            qu = t * 4 + hh
            pfx = f"Q{qu}"
            nbw = [f"Q{qu-2}z"] if qu >= 2 else ["BN3ydone"]
            nmm_pe2(pfx, qu, nb_of[pfx], nbw)
            lnexp_act(pfx, qu % 2,
                      [f"Q{qu-2}iv"] if qu >= 2 else [f"C{28+qu}iv"])

        def q_norm_b(t, hh):
            qu = t * 4 + hh
            pfx = f"Q{qu}"
            pmm_pe(pfx, qu, nb_of[pfx])
            pev_act(pfx, qu, nb_of[pfx], f"Q{qu-2}" if qu >= 2 else f"C{30+qu}")

        def q_norm_c(t, hh):
            qu = t * 4 + hh
            pfx = f"Q{qu}"
            invb_pe(pfx, qu % 2, nb_of[pfx])
            range_dve(pfx, qu, nb_of[pfx],
                      f"Q{qu-2}" if qu >= 2 else f"C{30+qu}")
            featw = [f"Qpr{(qu-2)*4+3}"] if qu >= 2 else []
            sin_act(pfx, qu, T["qp"][:, (qu % 2) * 512:(qu % 2) * 512 + 512],
                    featw)

        def q_pairs(t, hh):
            g, s = t // 4, t % 4
            qu = t * 4 + hh
            pfx = f"Q{qu}"
            nb = nb_of[pfx]
            qpv = T["qp"][:, (qu % 2) * 512:(qu % 2) * 512 + 512]
            def pr(b):
                pb = qu * 4 + b
                w = []
                if b == 0:
                    w.append(f"{pfx}sn")
                    if hh == 0:
                        w.append(f"R{t}")
                if pb >= 2:
                    w.append(f"Qbm{pb-2}")
                E.op("dve", lambda v, b=b, s=s: v.scalar_tensor_tensor(
                    T["prodb"][:, (b % 2) * 1024:(b % 2) * 1024 + 1024]
                    .rearrange("p (k t) -> p k t", k=2),
                    qpv.unsqueeze(1).broadcast_to([128, 2, 512]), 1.0,
                    blob[s % 2][:, (16 + 2 * b) * 512:(18 + 2 * b) * 512]
                    .rearrange("p (k t) -> p k t", k=2),
                    ALU.mult, ALU.mult), waits=w, inc="s_dve")
                E.snap(f"Qpr{pb}", "s_dve")
            def bm(b):
                pb = qu * 4 + b
                sp_ = spans[1 + b % 2]
                w = [f"Qpr{pb}"]
                if pb >= 2:
                    w.append(f"Qtm{pb-2}")
                for j in range(2):
                    E.op("pe", lambda t_, b=b, j=j, sp_=sp_: t_.matmul(
                        sp_[:, j * 512:(j + 1) * 512], P["ones128"][:],
                        T["prodb"][:, (b % 2) * 1024 + j * 512:
                                   (b % 2) * 1024 + (j + 1) * 512],
                        start=True, stop=True),
                        waits=(w if j == 0 else []), inc="s_pe")
                E.snap(f"Qbm{pb}", "s_pe")
            def tm(b):
                pb = qu * 4 + b
                sp_ = spans[1 + b % 2]
                w = [f"Qbm{pb}"]
                if pb >= 2:
                    w.append(f"Qim{pb-2}")
                E.op("dve", lambda v, b=b, sp_=sp_, s=s: v.scalar_tensor_tensor(
                    T["tsb"][:, (b % 2) * 1024:(b % 2) * 1024 + 1024]
                    .rearrange("p (k t) -> p k t", k=2),
                    sp_[:].rearrange("p (k t) -> p k t", k=2), 1.0,
                    blob[s % 2][:, (24 + 2 * b) * 512:(26 + 2 * b) * 512]
                    .rearrange("p (k t) -> p k t", k=2),
                    ALU.mult, ALU.mult), waits=w, inc="s_dve")
                E.snap(f"Qtm{pb}", "s_dve")
            def im(b):
                pb = qu * 4 + b
                w = [f"Qtm{pb}"]
                if b == 0:
                    w.append(f"{pfx}rs")
                for j in range(2):
                    E.op("pe", lambda t_, b=b, j=j, nb=nb: t_.matmul(
                        nb, P["ident16"][:],
                        T["tsb"][:, (b % 2) * 1024 + j * 512:
                                 (b % 2) * 1024 + (j + 1) * 512],
                        start=(b == 0 and j == 0), stop=(b == 3 and j == 1)),
                        waits=(w if j == 0 else []), inc="s_pe")
                E.snap(f"Qim{pb}", "s_pe")
            pr(0); bm(0); pr(1); bm(1); tm(0); im(0); pr(2); bm(2); tm(1)
            im(1); pr(3); bm(3); tm(2); im(2); tm(3); im(3)
            E.snap(f"{pfx}aL", "s_pe")
            if hh == 3:
                E.snap(f"QdL{t}", "s_dve")

        def q_zev(t, hh):
            g, s = t // 4, t % 4
            qu = t * 4 + hh
            pfx = f"Q{qu}"
            nb = nb_of[pfx]
            w = [f"{pfx}aL"]
            if t == 0 and hh == 0:
                w.append("BN3ydone")
            if s == 0 and hh == 0 and g >= 2:
                w.append(f"OMl{g-2}")
            zgv = zgbits[:, (g % 2) * 8192:(g % 2) * 8192 + 8192]
            E.op("act", lambda a, hh=hh, s=s, nb=nb, zgv=zgv: a.activation(
                zgv.rearrange("p (j t) -> p j t", j=16)
                [:, :, 128 * hh + 32 * s:128 * hh + 32 * s + 32],
                nb.rearrange("p (sg j) -> p j sg", j=16),
                AF.Copy, scale=1.0 / 64.0),
                waits=w, inc="s_act")
            E.snap(f"{pfx}z", "s_act")
            if s == 3 and hh == 3:
                E.snap(f"ZG{g}", "s_act")

        def o_loads(go, c):
            lw = []
            if c >= 2:
                lw.append(f"OM{go}_{c-2}")
            elif go > 0:
                lw.append(f"OM{go-1}_{14+c}")
            E.op("pool", lambda g, c=c: g.dma_start(
                out=T["wob"][:, (c % 2) * 2048:(c % 2 + 1) * 2048]
                .rearrange("p (k n) -> p k n", k=KC),
                in_=D["wo_T"][:, c * 128:(c + 1) * 128]
                .rearrange("(k p) n -> p k n", p=128)),
                waits=lw, inc=f"s_wo{c%2}", dma=True)
            E.snap(f"WO{go}_{c}", f"s_wo{c%2}")
            xw = []
            if c >= 2:
                xw.append(f"OX{go}_{c-2}")
            elif go > 0:
                xw.append(f"OX{go-1}_{14+c}")
            E.op("sp", lambda sy, c=c, go=go: sy.dma_start(
                out=T["x1o"][:, (c % 2) * 512:(c % 2 + 1) * 512],
                in_=x13[:, c, go * 512:(go + 1) * 512]),
                waits=xw, inc=f"s_o{c%2}", dma=True)
            E.snap(f"XO{go}_{c}", f"s_o{c%2}")

        def o_mm(go, c, bankw):
            w = [f"WO{go}_{c}"] + bankw
            if c == 0:
                w.append(f"ZG{go}")
            zgo = zgbits[:, (go % 2) * 8192:(go % 2) * 8192 + 8192]
            for k in range(KC):
                E.op("pe", lambda t_, c=c, k=k, zgo=zgo: t_.matmul(
                    banks[c % 2],
                    T["wob"][:, (c % 2) * 2048 + k * 128:
                             (c % 2) * 2048 + (k + 1) * 128],
                    zgo[:, k * 512:(k + 1) * 512],
                    start=(k == 0), stop=(k == KC - 1)),
                    waits=(w if k == 0 else []),
                    inc=("s_pe" if k == KC - 1 else None))
            E.snap(f"OM{go}_{c}", "s_pe")
            if c == 15:
                E.snap(f"OMl{go}", "s_pe")

        def o_tail(go, c):
            w = [f"OM{go}_{c}", f"XO{go}_{c}"]
            if c >= 2:
                w.append(f"OS{go}_{c-2}")
            elif go > 0:
                w.append(f"OS{go-1}_{14+c}")
            E.op("dve", lambda v, c=c: v.scalar_tensor_tensor(
                T["x2o"][:, (c % 2) * 512:(c % 2 + 1) * 512], banks[c % 2],
                P["bo"][:, c:c + 1], T["x1o"][:, (c % 2) * 512:(c % 2 + 1) * 512],
                ALU.add, ALU.add), waits=w, inc="s_dve")
            E.snap(f"OX{go}_{c}", "s_dve")
            E.op("sp", lambda sy, c=c, go=go: sy.dma_start(
                out=dst3[:, c, go * 512:(go + 1) * 512],
                in_=T["x2o"][:, (c % 2) * 512:(c % 2 + 1) * 512]),
                waits=[f"OX{go}_{c}"], inc=f"s_x2{c%2}", dma=True)
            E.snap(f"OS{go}_{c}", f"s_x2{c%2}")

        for t in range(16):
            g, s = t // 4, t % 4
            go = g - 1
            if t >= 1 and t + 1 <= 15:
                reload(t + 1)
            if go >= 0:
                o_loads(go, s * 4)
                o_loads(go, s * 4 + 1)
            if t == 0:
                b0w = [f"C{31}ra", f"C{31}d1"]
                b1w = [f"Cev{31}"]
            elif go >= 1 or (go == 0 and s >= 1):
                pc = (s - 1) * 4 if s >= 1 else (3 * 4)
                gop = go if s >= 1 else go - 1
                b0w = [f"OX{gop}_{pc+2}"]
                b1w = [f"OX{gop}_{pc+3}"]
            else:
                qup = (t - 1) * 4
                b0w = [f"Q{qup+2}ra", f"Q{qup+2}d1"]
                b1w = [f"Q{qup+3}ra", f"Q{qup+3}d1"]
            c0, c1, c2, c3 = s * 4, s * 4 + 1, s * 4 + 2, s * 4 + 3
            qu = t * 4
            q_proj(t, 0, b0w)
            if s == 3 and g * 4 + 5 <= 15:
                qw_load(g * 4 + 5, [f"Q{qu}pj"])
            q_proj(t, 1, b1w)
            if s == 3 and g * 4 + 6 <= 15:
                qw_load(g * 4 + 6, [f"Q{qu+1}pj"])
            q_rope(t, 0)
            q_norm_a(t, 0)
            q_proj(t, 2, [f"Q{qu}ra", f"Q{qu}d1"])
            if s == 3 and g * 4 + 7 <= 15:
                qw_load(g * 4 + 7, [f"Q{qu+2}pj"])
            q_rope(t, 1)
            q_norm_b(t, 0)
            q_norm_a(t, 1)
            q_proj(t, 3, [f"Q{qu+1}ra", f"Q{qu+1}d1"])
            if s == 3 and g * 4 + 8 <= 15:
                qw_load(g * 4 + 8, [f"Q{qu+3}pj"])
            q_rope(t, 2)
            q_norm_c(t, 0)
            q_norm_b(t, 1)
            if go >= 0:
                o_mm(go, c0, [f"Q{qu+2}ra", f"Q{qu+2}d1"])
            q_rope(t, 3)
            q_norm_c(t, 1)
            q_pairs(t, 0)
            q_zev(t, 0)
            q_norm_a(t, 2)
            if go >= 0:
                o_mm(go, c1, [f"Q{qu+3}ra", f"Q{qu+3}d1"])
                o_tail(go, c0)
                o_loads(go, c2)
            q_norm_b(t, 2)
            q_pairs(t, 1)
            q_zev(t, 1)
            q_norm_a(t, 3)
            if go >= 0:
                o_mm(go, c2, [f"OX{go}_{c0}"])
                o_tail(go, c1)
                o_loads(go, c3)
            q_norm_c(t, 2)
            q_norm_b(t, 3)
            q_pairs(t, 2)
            q_zev(t, 2)
            if go >= 0:
                o_mm(go, c3, [f"OX{go}_{c1}"])
                o_tail(go, c2)
            q_norm_c(t, 3)
            q_pairs(t, 3)
            q_zev(t, 3)
            if go >= 0:
                o_tail(go, c3)

        for c in range(16):
            o_loads(3, c)
            if c == 0:
                bw = [f"OX{2}_{14}"]
            elif c == 1:
                bw = [f"OX{2}_{15}"]
            else:
                bw = [f"OX{3}_{c-2}"]
            o_mm(3, c, bw)
            o_tail(3, c)
        E.snap("BX0", "s_x20")
        E.snap("BX1", "s_x21")

    if "A" in phases:
        with ExitStack() as es:
            T = alloc_ffn(es, "fA_")
            dstA = D["x1s"] if ("B" in phases or "C" in phases) else D["out_fm"]
            with nc.Block("phaseA") as blk:
                run_block(blk, lambda: sched_ffn("A", T, D["x_fm"], dstA, []))

    if "B" in phases:
        with ExitStack() as es:
            T = alloc_attn_B(es)
            dstB = D["x2s"] if "C" in phases else D["out_fm"]
            with nc.Block("phaseB") as blk:
                run_block(blk, lambda: sched_attn_B(T, dstB))

    if "C" in phases:
        with ExitStack() as es:
            T = alloc_ffn(es, "fC_")
            srcC = D["x2s"] if "B" in phases else (
                D["x1s"] if "A" in phases else D["x_fm"])
            def schedC():
                start = []
                if "B" in phases:
                    start = [("s_x20", E.snaps["BX0"][1]),
                             ("s_x21", E.snaps["BX1"][1])]
                elif "A" in phases:
                    start = [("s_st", E.snaps["Aend"][1])]
                sched_ffn("C", T, srcC, D["out_fm"], start)
            with nc.Block("phaseC") as blk:
                run_block(blk, schedC)

    def sched_final():
        E.wait("sp", [("s_st", E.cnt["s_st"]),
                      ("s_x20", E.cnt["s_x20"]), ("s_x21", E.cnt["s_x21"])])
    with nc.Block("final") as blk:
        run_block(blk, sched_final)

    sem_es.close()
    pes.close()
    return nc


def kernel(**inputs):
    H = _host_prep(inputs)
    nc = build_nc()
    x = np.asarray(inputs["L0_attn_hidden_states"], np.float32)
    in_maps = []
    for c in range(8):
        m = dict(H)
        m["x_fm"] = np.ascontiguousarray(x[c].T)
        in_maps.append(m)
    res = run_bass_kernel_spmd(nc, in_maps, core_ids=list(range(8)))
    out = np.stack([np.asarray(res.results[c]["out_fm"]).T for c in range(8)])
    return out.astype(np.float32)
